# version 02
# baseline (speedup 1.0000x reference)
import numpy as np
from contextlib import ExitStack

import concourse.bass as bass
import concourse.mybir as mybir
from concourse.bass_utils import run_bass_kernel_spmd

F32 = mybir.dt.float32
BF16 = mybir.dt.bfloat16
I32 = mybir.dt.int32
AF = mybir.ActivationFunctionType
ALU = mybir.AluOpType
AX = mybir.AxisListType

T = 2048
D = 1024
NPAIR = 8
DI = 2048
NST = 16
NE = 8
EPS = 1e-5
TC = 256
NCH = T // TC

DEBUG_DUMPS = []
STOP_AFTER = None


class K:

    ENGS = ("tensor", "vector", "scalar", "gpsimd", "sync")

    def __init__(self, nc):
        self.nc = nc
        self.q = {e: [] for e in self.ENGS}
        self.sems = {}
        self.c = {}
        self.maxwait = {}

    def sem(self, st, name):
        self.sems[name] = st.enter_context(self.nc.semaphore(name))
        self.c[name] = 0
        self.maxwait[name] = 0

    def _op(self, eng, fn, inc, k):
        n = None
        if inc is not None:
            self.c[inc] += k
            n = self.c[inc]
            h = self.sems[inc]
            self.q[eng].append(lambda e, fn=fn, h=h, k=k: fn(e).then_inc(h, k))
        else:
            self.q[eng].append(lambda e, fn=fn: fn(e))
        return n

    def t(self, fn, inc=None, k=1):
        return self._op("tensor", fn, inc, k)

    def v(self, fn, inc=None, k=1):
        return self._op("vector", fn, inc, k)

    def s(self, fn, inc=None, k=1):
        return self._op("scalar", fn, inc, k)

    def g(self, fn, inc=None, k=1):
        return self._op("gpsimd", fn, inc, k)

    def y(self, fn, inc=None, k=1):
        return self._op("sync", fn, inc, k)

    def w(self, eng, sem, val):
        if val is None or val <= 0:
            return
        h = self.sems[sem]
        self.maxwait[sem] = max(self.maxwait[sem], val)
        self.q[eng].append(lambda e, h=h, v=val: e.wait_ge(h, v))

    def barrier(self):
        vals = {sn: self.c[sn] for sn in self.sems}
        for e in self.ENGS:
            for sn, v in vals.items():
                self.w(e, sn, v)

    def flush(self):
        for name in self.sems:
            assert self.maxwait[name] <= self.c[name], (
                f"sem {name}: wait {self.maxwait[name]} > inc total {self.c[name]}")
        with self.nc.Block() as b:
            for eng in self.ENGS:
                ops = self.q[eng]
                if ops:
                    getattr(b, eng)(lambda e, ops=ops: [f(e) for f in ops])
        self.q = {e: [] for e in self.ENGS}


def build():
    nc = bass.Bass("TRN2", target_bir_lowering=False)
    din = {}

    def inp(name, shape, dtype=F32):
        din[name] = nc.dram_tensor(name, list(shape), dtype,
                                   kind="ExternalInput").ap()
        return din[name]

    x_in = inp("x", (T, D))
    ident_in = inp("ident", (128, 128))
    oh32b_in = inp("oh32f", (96, 32 * 128))
    oh8f_in = inp("oh8f", (8, 8 * 128))

    wq_in = inp("l0_wq", (D, D))
    wk_in = inp("l0_wk", (D, D))
    wv_in = inp("l0_wv", (D, D))
    wo_in = inp("l0_wo", (D, D))
    bo_in = inp("l0_bo_pp", (128, 8))

    gup_in = inp("moe_gup", (NE, D, 2048))
    gub_in = inp("moe_gub_pp", (128, NE * 16))
    down_in = inp("moe_down", (NE, 1024, D))
    dbp_in = inp("moe_db_pp", (128, NE * 8))
    ridx_in = inp("moe_ridx", (128, 32), I32)
    rw_in = inp("moe_rw", (128, 16 * 8))
    mask_in = inp("moe_mask", (128, 1))

    mlnw_in = inp("m_lnw_pp", (128, 8))
    mlnb_in = inp("m_lnb_pp", (128, 8))
    minp_in = inp("m_inproj_t", (D, 2 * DI))
    mcw_in = inp("m_convw_pp", (128, 64))
    mcb_in = inp("m_convb_pp", (128, 16))
    mxp_in = inp("m_xproj_t", (DI, 96))
    mdt_in = inp("m_dtproj_t", (64, DI))
    mdtb_in = inp("m_dtb_pp", (128, 16))
    malog_in = inp("m_alog_pp", (128, 256))
    mD_in = inp("m_d_pp", (128, 16))
    mout_in = inp("m_outproj_t", (DI, D))

    fgw_in = inp("f_gw", (D, NE))
    fgb_in = inp("f_gb_t", (128, NE))
    fw1_in = inp("f_w1", (NE, D, 2048))
    fw2_in = inp("f_w2", (NE, D, 2048))
    fwg_in = inp("f_wg", (NE, 2048, 2048))
    fw3_in = inp("f_w3", (NE, 2048, D))
    fb1_in = inp("f_b1_pp", (128, NE * 16))
    fb2_in = inp("f_b2_pp", (128, NE * 16))
    fbg_in = inp("f_bg_pp", (128, NE * 16))
    fb3_in = inp("f_b3_pp", (128, NE * 8))

    scr_x1 = nc.dram_tensor("scr_x1", [128, 16384], F32).ap()
    scr_rows = nc.dram_tensor("scr_rows", [1, 4096], F32).ap()
    out_d = nc.dram_tensor("out", [T, D], F32, kind="ExternalOutput").ap()
    dbg = {}
    for nm in DEBUG_DUMPS:
        dbg[nm] = nc.dram_tensor("dbg_" + nm, [128, 16384], F32,
                                 kind="ExternalOutput").ap()

    with ExitStack() as st:
        sbt = lambda n, sh, dt=F32: st.enter_context(
            nc.sbuf_tensor(n, list(sh), dt))
        pst = lambda n: st.enter_context(nc.psum_tensor(n, [128, 512], F32))

        xslot = sbt("xslot", (128, 16384), BF16)
        oh32b = sbt("s_oh32", (96, 32 * 128), BF16)
        oh8f = sbt("s_oh8", (8, 8 * 128))
        identf = sbt("identf", (128, 128))
        identb = sbt("identb", (128, 128), BF16)
        onesf = sbt("onesf", (128, 2))
        onesb = sbt("onesb", (128, 2), BF16)
        onesrow = sbt("onesrow", (128, 128))
        epsc = sbt("epsc", (128, 1))
        bo_pp = sbt("bo_pp", (128, 8))
        gub_pp = sbt("gub_pp", (128, NE * 16))
        db_pp = sbt("db_pp", (128, NE * 8))
        ridx_i = sbt("ridx_i", (128, 32), I32)
        ridx_f = sbt("ridx_f", (128, 32))
        rw_t = sbt("rw_t", (128, 16 * 8))
        mask_pp = sbt("mask_pp", (128, 1))
        m_lnw = sbt("m_lnw", (128, 8))
        m_lnb = sbt("m_lnb", (128, 8))
        m_cw = sbt("m_cw", (128, 64))
        m_cb = sbt("m_cb", (128, 16))
        m_dtb = sbt("m_dtb", (128, 16))
        m_app = sbt("m_app", (128, 256))
        m_dpp = sbt("m_dpp", (128, 16))
        f_gb = sbt("f_gb", (128, NE))
        f_b1 = sbt("f_b1", (128, NE * 16))
        f_b2 = sbt("f_b2", (128, NE * 16))
        f_bg = sbt("f_bg", (128, NE * 16))
        f_b3 = sbt("f_b3", (128, NE * 8))

        pA, pB, pC, pD, pE, pR = (pst("pA"), pst("pB"), pst("pC"),
                                  pst("pD"), pst("pE"), pst("pR"))
        pO = st.enter_context(nc.psum_tensor("pO", [65, 512], F32))

        kk = K(nc)
        for s_ in ("dW", "dX", "T", "V", "S", "G", "B"):
            kk.sem(st, s_)

        def stop_here(tag):
            return STOP_AFTER == tag

        kk.g(lambda e: e.dma_start(oh32b[:, :], oh32b_in), inc="dW", k=16)
        kk.w("tensor", "dW", kk.c["dW"])
        for dst, src in [
            (identf[:, :], ident_in),
            (oh8f[:, :], oh8f_in), (bo_pp[:, :], bo_in),
            (gub_pp[:, :], gub_in), (db_pp[:, :], dbp_in),
            (ridx_i[:, :], ridx_in), (rw_t[:, :], rw_in),
            (mask_pp[:, :], mask_in), (m_lnw[:, :], mlnw_in),
            (m_lnb[:, :], mlnb_in), (m_cw[:, :], mcw_in), (m_cb[:, :], mcb_in),
            (m_dtb[:, :], mdtb_in), (m_app[:, :], malog_in),
            (m_dpp[:, :], mD_in), (f_gb[:, :], fgb_in), (f_b1[:, :], fb1_in),
            (f_b2[:, :], fb2_in), (f_bg[:, :], fbg_in), (f_b3[:, :], fb3_in),
        ]:
            kk.y(lambda e, d=dst, s_=src: e.dma_start(d, s_), inc="dX", k=16)
        cdma = kk.c["dX"]
        kk.w("vector", "dX", cdma)
        kk.w("scalar", "dX", cdma)
        kk.v(lambda e: e.memset(onesf[:, :], 1.0))
        kk.v(lambda e: e.memset(onesrow[:, :], 1.0))
        kk.v(lambda e: e.memset(epsc[:, :], EPS))
        kk.v(lambda e: e.memset(onesb[:, :], 1.0))
        kk.v(lambda e: e.tensor_copy(identb[:, :], identf[:, :]))
        kk.v(lambda e: e.tensor_copy(ridx_f[:, :], ridx_i[:, :]), inc="V")
        kk.s(lambda e: e.activation(m_app[:, :], m_app[:, :], AF.Exp), inc="S")
        kk.w("vector", "S", kk.c["S"])
        kk.v(lambda e: e.tensor_scalar(m_app[:, :], m_app[:, :], -1.0, None,
                                       ALU.mult), inc="V")

        accA = sbt("accA", (128, 16384))

        with ExitStack() as ph:
            xt = [ph.enter_context(nc.sbuf_tensor(f"p1x{i}", [128, D], F32))
                  for i in range(2)]
            sqt = ph.enter_context(nc.sbuf_tensor("p1sq", [128, D], F32))
            stt = ph.enter_context(nc.sbuf_tensor("p1st", [128, 96], F32))
            xnb = ph.enter_context(nc.sbuf_tensor("p1xn", [128, D], BF16))

            used = [0, 0]
            for tt in range(16):
                b = tt % 2
                kk.w("sync", "S", used[b])
                kk.y(lambda e, b=b, tt=tt: e.dma_start(
                    xt[b][:, :], x_in[tt * 128:(tt + 1) * 128, :]), inc="dX", k=16)
                ld = kk.c["dX"]
                kk.w("vector", "dX", ld)
                kk.v(lambda e, b=b, tt=tt: e.reduce_sum(
                    stt[:, tt:tt + 1], xt[b][:, :], AX.X), inc="V")
                vred = kk.c["V"]
                kk.w("scalar", "dX", ld)
                kk.w("scalar", "V", vred)
                kk.s(lambda e, b=b, tt=tt: e.activation(
                    sqt[:, :], xt[b][:, :], AF.Square,
                    accum_out=stt[:, 16 + tt:17 + tt]), inc="S")
                used[b] = kk.c["S"]
            kk.w("vector", "S", kk.c["S"])
            kk.v(lambda e: e.tensor_scalar(stt[:, 32:48], stt[:, 0:16], 1.0 / D,
                                           None, ALU.mult))
            kk.v(lambda e: e.tensor_scalar(stt[:, 48:64], stt[:, 16:32], 1.0 / D,
                                           None, ALU.mult), inc="V")
            kk.w("scalar", "V", kk.c["V"])
            kk.s(lambda e: e.activation(stt[:, 64:80], stt[:, 32:48], AF.Square),
                 inc="S")
            kk.w("vector", "S", kk.c["S"])
            kk.v(lambda e: e.tensor_sub(stt[:, 48:64], stt[:, 48:64],
                                        stt[:, 64:80]), inc="V")
            kk.w("scalar", "V", kk.c["V"])
            kk.s(lambda e: e.activation(stt[:, 64:80], stt[:, 48:64], AF.Sqrt,
                                        bias=epsc[:, 0:1]), inc="S")
            kk.w("vector", "S", kk.c["S"])
            kk.v(lambda e: e.reciprocal(stt[:, 80:96], stt[:, 64:80]), inc="V")
            stats_ok = kk.c["V"]
            passA_S = kk.c["S"]

            used = [0, 0]
            tev = [0, 0]
            last_tp = 0
            for tt in range(16):
                b = tt % 2
                kk.w("sync", "V", used[b])
                if tt < 2:
                    kk.w("sync", "S", passA_S)
                    kk.w("sync", "V", stats_ok)
                kk.y(lambda e, b=b, tt=tt: e.dma_start(
                    xt[b][:, :], x_in[tt * 128:(tt + 1) * 128, :]), inc="dX", k=16)
                ld = kk.c["dX"]
                kk.w("vector", "dX", ld)
                kk.w("vector", "V", stats_ok)
                kk.w("vector", "T", last_tp)
                kk.v(lambda e, b=b, tt=tt: e.tensor_scalar(
                    xnb[:, :], xt[b][:, :], stt[:, 32 + tt:33 + tt],
                    stt[:, 80 + tt:81 + tt], ALU.subtract, ALU.mult), inc="V")
                xn_ok = kk.c["V"]
                used[b] = xn_ok
                for r in range(8):
                    pt = (pA, pB)[r % 2]
                    ptb = pt.ap().bitcast(BF16)[:, 0:128]
                    kk.w("tensor", "V", xn_ok)
                    kk.w("tensor", "V", tev[r % 2])
                    kk.t(lambda e, ptb=ptb, r=r: e.transpose(
                        ptb, xnb[:, r * 128:(r + 1) * 128], identb[:, :]),
                        inc="T")
                    td = kk.c["T"]
                    kk.w("vector", "T", td)
                    kk.v(lambda e, ptb=ptb, r=r, tt=tt: e.tensor_copy(
                        xslot[:, r * 2048 + tt * 128: r * 2048 + (tt + 1) * 128],
                        ptb), inc="V")
                    tev[r % 2] = kk.c["V"]
                last_tp = kk.c["T"]
            kk.barrier()
            if "p1" in dbg:
                kk.g(lambda e: e.dma_start(dbg["p1"], xslot[:, :]), inc="dW", k=16)
                kk.w("gpsimd", "dW", kk.c["dW"])
                kk.barrier()
            kk.flush()

        if not stop_here("p1"):
          with ExitStack() as ph:
            wqkv = [ph.enter_context(nc.sbuf_tensor(f"p2w{i}", [128, 4096], BF16))
                    for i in range(2)]
            qp = [ph.enter_context(nc.sbuf_tensor(f"p2q{i}", [128, T], BF16))
                  for i in range(2)]
            kp = [ph.enter_context(nc.sbuf_tensor(f"p2k{i}", [128, T], BF16))
                  for i in range(2)]
            vp = [ph.enter_context(nc.sbuf_tensor(f"p2v{i}", [128, 2080], BF16))
                  for i in range(2)]
            sqb = ph.enter_context(nc.sbuf_tensor("p2sqb", [128, T], BF16))
            ptl = [ph.enter_context(nc.sbuf_tensor(f"p2p{i}", [128, 512], BF16))
                   for i in range(2)]
            aTp = ph.enter_context(nc.sbuf_tensor("p2aT", [128, T], BF16))
            aTt = ph.enter_context(nc.sbuf_tensor("p2aTt", [64, T], BF16))
            rbx = ph.enter_context(nc.sbuf_tensor("p2rb", [64, 512], F32))
            rip = ph.enter_context(nc.sbuf_tensor("p2rip", [128, 512], F32))
            nrm = ph.enter_context(nc.sbuf_tensor("p2nrm", [1, 16], F32))
            cbias = ph.enter_context(nc.sbuf_tensor("p2cb", [128, 2], F32))

            def qkvo_load(p, buf):
                for j, w_ in enumerate((wq_in, wk_in, wv_in)):
                    dst = wqkv[buf][:, j * 1024:(j + 1) * 1024].rearrange(
                        "q (kc m) -> q kc m", kc=8)
                    kk.g(lambda e, dst=dst, w_=w_, p=p: e.dma_start(
                        dst, w_.rearrange("(kc q) m -> q kc m", q=128)
                        [:, :, p * 128:(p + 1) * 128]), inc="dW", k=16)
                kk.g(lambda e, buf=buf, p=p: e.dma_start(
                    wqkv[buf][:, 3 * 1024:4 * 1024],
                    wo_in[p * 128:(p + 1) * 128, :]), inc="dW", k=16)
                return kk.c["dW"]

            wld = [qkvo_load(0, 0), 0]
            pairT = [0, 0]
            qk_ev = [0, 0]
            v_ev = [0, 0]
            exp_ev = [0, 0]
            pv_mm = [0, 0]
            op_ev = [0, 0]
            aT_used_V = 0
            aT_dma = 0

            for p in range(NPAIR):
                buf = p % 2
                if p + 1 < NPAIR:
                    kk.w("gpsimd", "T", pairT[(p + 1) % 2])
                    wld[(p + 1) % 2] = qkvo_load(p + 1, (p + 1) % 2)

                kk.w("tensor", "dW", wld[buf])

                for which, wofs, dst in (("q", 0, qp), ("k", 1, kp)):
                    for tg in range(4):
                        pt = (pA, pB)[tg % 2]
                        kk.w("tensor", "S", qk_ev[tg % 2])
                        for kc in range(8):
                            kk.t(lambda e, pt=pt, wofs=wofs, kc=kc, tg=tg, buf=buf:
                                 e.matmul(pt[:, :],
                                          wqkv[buf][:, wofs * 1024 + kc * 128:
                                                    wofs * 1024 + (kc + 1) * 128],
                                          xslot[:, kc * 2048 + tg * 512:
                                                kc * 2048 + (tg + 1) * 512],
                                          start=(kc == 0), stop=(kc == 7)),
                                 inc="T" if kc == 7 else None)
                        mm = kk.c["T"]
                        kk.w("scalar", "T", mm)
                        kk.s(lambda e, pt=pt, tg=tg, buf=buf, dst=dst: e.copy(
                            dst[buf][:, tg * 512:(tg + 1) * 512], pt[:, :]),
                            inc="S")
                        qk_ev[tg % 2] = kk.c["S"]
                qk_done = kk.c["S"]

                for tt in range(16):
                    pt = (pC, pD)[tt % 2]
                    kk.w("tensor", "V", v_ev[tt % 2])
                    for kc in range(8):
                        kk.t(lambda e, pt=pt, kc=kc, tt=tt, buf=buf: e.matmul(
                            pt[:, 0:128],
                            xslot[:, kc * 2048 + tt * 128: kc * 2048 + (tt + 1) * 128],
                            wqkv[buf][:, 2 * 1024 + kc * 128:
                                      2 * 1024 + (kc + 1) * 128],
                            start=(kc == 0), stop=(kc == 7)),
                            inc="T" if kc == 7 else None)
                    mm = kk.c["T"]
                    kk.w("vector", "T", mm)
                    for hl in range(2):
                        kk.v(lambda e, pt=pt, tt=tt, hl=hl, buf=buf: e.tensor_copy(
                            vp[buf][:, tt * 130 + hl * 65: tt * 130 + hl * 65 + 64],
                            pt[:, hl * 64: hl * 64 + 64]),
                            inc="V" if hl == 1 else None)
                    v_ev[tt % 2] = kk.c["V"]
                for hl in range(2):
                    kk.v(lambda e, hl=hl, buf=buf: e.memset(
                        vp[buf].ap().rearrange("q (tt c) -> q tt c", tt=16)
                        [:, :, hl * 65 + 64: hl * 65 + 65], 1.0), inc="V")
                v_done = kk.c["V"]

                for j, srcb in ((0, qp), (8, kp)):
                    kk.w("scalar", "S", qk_done)
                    kk.w("scalar", "T", kk.c["T"])
                    kk.s(lambda e, srcb=srcb, buf=buf: e.activation(
                        sqb[:, :], srcb[buf][:, :], AF.Square), inc="S")
                    sq_ok = kk.c["S"]
                    for hl in range(2):
                        for tg in range(4):
                            kk.w("tensor", "S", sq_ok)
                            if j + hl + tg > 0:
                                kk.w("tensor", "V", kk.c["V"])
                            kk.t(lambda e, hl=hl, tg=tg: e.matmul(
                                pR[0:1, :], onesb[hl * 64: hl * 64 + 64, 0:1],
                                sqb[hl * 64: hl * 64 + 64, tg * 512:(tg + 1) * 512],
                                start=True, stop=True), inc="T")
                            mt = kk.c["T"]
                            kk.w("vector", "T", mt)
                            kk.v(lambda e, j=j, hl=hl, tg=tg: e.reduce_max(
                                nrm[0:1, j + hl * 4 + tg: j + hl * 4 + tg + 1],
                                pR[0:1, :], AX.X), inc="V")
                kk.w("vector", "V", kk.c["V"])
                kk.v(lambda e: e.reduce_max(nrm[0:1, 0:1], nrm[0:1, 0:4], AX.X))
                kk.v(lambda e: e.reduce_max(nrm[0:1, 1:2], nrm[0:1, 4:8], AX.X))
                kk.v(lambda e: e.reduce_max(nrm[0:1, 2:3], nrm[0:1, 8:12], AX.X))
                kk.v(lambda e: e.reduce_max(nrm[0:1, 3:4], nrm[0:1, 12:16],
                                            AX.X), inc="V")
                kk.w("vector", "V", kk.c["V"])
                kk.v(lambda e: e.tensor_add(nrm[0:1, 4:5], nrm[0:1, 0:1],
                                            nrm[0:1, 2:3]))
                kk.v(lambda e: e.tensor_add(nrm[0:1, 5:6], nrm[0:1, 1:2],
                                            nrm[0:1, 3:4]), inc="V")
                kk.w("vector", "V", kk.c["V"])
                kk.v(lambda e: e.tensor_scalar(nrm[0:1, 4:6], nrm[0:1, 4:6],
                                               -0.0625, None, ALU.mult), inc="V")
                nrm_ok = kk.c["V"]
                kk.w("tensor", "V", nrm_ok)
                for hl in range(2):
                    kk.t(lambda e, hl=hl: e.matmul(
                        pR[:, 0:1], oh8f[0:1, 0:128], nrm[0:1, 4 + hl:5 + hl],
                        start=True, stop=True), inc="T")
                    mt = kk.c["T"]
                    kk.w("scalar", "T", mt)
                    kk.s(lambda e, hl=hl: e.copy(cbias[:, hl:hl + 1], pR[:, 0:1]),
                         inc="S")
                cb_ok = kk.c["S"]
                kk.w("scalar", "S", cb_ok)

                kk.w("tensor", "S", cb_ok)
                kk.w("tensor", "V", v_done)
                kk.w("vector", "dX", aT_dma)
                for hl in range(2):
                    hb = hl * 64
                    for tg in range(4):
                        for kc in range(16):
                            pss = (pA, pB)[kc % 2]
                            kk.w("tensor", "S", exp_ev[kc % 2])
                            kk.t(lambda e, pss=pss, hb=hb, kc=kc, tg=tg, buf=buf:
                                 e.matmul(
                                     pss[:, :],
                                     kp[buf][hb:hb + 64, kc * 128:(kc + 1) * 128],
                                     qp[buf][hb:hb + 64, tg * 512:(tg + 1) * 512],
                                     start=True, stop=True), inc="T")
                            smm = kk.c["T"]
                            kk.w("scalar", "T", smm)
                            kk.w("scalar", "T", pv_mm[kc % 2])
                            kk.s(lambda e, pss=pss, kc=kc, hl=hl: e.activation(
                                ptl[kc % 2][:, :], pss[:, :], AF.Exp,
                                bias=cbias[:, hl:hl + 1], scale=0.125), inc="S")
                            exp_ev[kc % 2] = kk.c["S"]
                            kk.w("tensor", "S", exp_ev[kc % 2])
                            if kc == 0:
                                kk.w("tensor", "V", aT_used_V)
                            kk.t(lambda e, kc=kc, hl=hl, buf=buf: e.matmul(
                                pO[:, :],
                                vp[buf][:, kc * 130 + hl * 65:
                                        kc * 130 + hl * 65 + 65],
                                ptl[kc % 2][:, :],
                                start=(kc == 0), stop=(kc == 15),
                                skip_group_check=True), inc="T")
                            pv_mm[kc % 2] = kk.c["T"]
                        pv_done = kk.c["T"]
                        kk.w("vector", "T", pv_done)
                        kk.v(lambda e: e.reciprocal(rip[64:65, :], pO[64:65, :]),
                             inc="V")
                        rip_ok = kk.c["V"]
                        kk.w("tensor", "V", rip_ok)
                        kk.t(lambda e: e.matmul(pR[0:64, :], onesrow[64:65, 0:64],
                                                rip[64:65, :], start=True,
                                                stop=True), inc="T")
                        rmm = kk.c["T"]
                        kk.w("scalar", "T", rmm)
                        kk.s(lambda e: e.copy(rbx[:, :], pR[0:64, :]), inc="S")
                        rev = kk.c["S"]
                        kk.w("vector", "S", rev)
                        if hl == 0:
                            dst = aTp[0:64, tg * 512:(tg + 1) * 512]
                        else:
                            dst = aTt[0:64, tg * 512:(tg + 1) * 512]
                        kk.v(lambda e, dst=dst: e.tensor_mul(
                            dst, pO[0:64, :], rbx[:, :]), inc="V")
                        aT_used_V = kk.c["V"]
                aT_done = kk.c["V"]
                kk.w("sync", "V", aT_done)
                kk.y(lambda e: e.dma_start(aTp[64:128, :], aTt[0:64, :]),
                     inc="dX", k=16)
                aT_dma = kk.c["dX"]

                kk.w("tensor", "V", aT_done)
                kk.w("tensor", "dX", aT_dma)
                for dc in range(8):
                    for tg in range(4):
                        i = dc * 4 + tg
                        pt = (pC, pD)[i % 2]
                        kk.w("tensor", "V", op_ev[i % 2])
                        kk.t(lambda e, pt=pt, dc=dc, tg=tg, buf=buf: e.matmul(
                            pt[:, :],
                            wqkv[buf][:, 3 * 1024 + dc * 128:
                                      3 * 1024 + (dc + 1) * 128],
                            aTp[:, tg * 512:(tg + 1) * 512],
                            start=True, stop=True), inc="T")
                        omm = kk.c["T"]
                        dst = accA[:, dc * 2048 + tg * 512:
                                   dc * 2048 + (tg + 1) * 512]
                        if p == 0:
                            kk.w("vector", "T", omm)
                            kk.v(lambda e, pt=pt, dc=dc, dst=dst: e.tensor_scalar(
                                dst, pt[:, :], bo_pp[:, dc:dc + 1], None,
                                ALU.add), inc="V")
                        else:
                            kk.w("vector", "T", omm)
                            kk.v(lambda e, pt=pt, dst=dst: e.tensor_add(
                                dst, dst, pt[:, :]), inc="V")
                        op_ev[i % 2] = kk.c["V"]
                pairT[buf] = kk.c["T"]

            kk.barrier()
            if "p2" in dbg:
                kk.y(lambda e: e.dma_start(dbg["p2"], accA[:, :]), inc="dX", k=16)
                kk.w("sync", "dX", kk.c["dX"])
                kk.barrier()
            kk.flush()

        if not stop_here("p1") and not stop_here("p2"):
          with ExitStack() as ph:
            gup_sb = ph.enter_context(nc.sbuf_tensor("p3gup", [128, 16384], BF16))
            down_sb = ph.enter_context(nc.sbuf_tensor("p3dwn", [128, 8192], BF16))
            glu_sb = ph.enter_context(nc.sbuf_tensor("p3glu", [128, 16384], BF16))
            cmat = ph.enter_context(nc.sbuf_tensor("p3cm", [128, 128], F32))
            cTt = ph.enter_context(nc.sbuf_tensor("p3cT", [8, 2048], F32))
            gbuf = ph.enter_context(nc.sbuf_tensor("p3g", [128, 512], F32))
            sgm = ph.enter_context(nc.sbuf_tensor("p3s", [128, 512], F32))
            scr = ph.enter_context(nc.sbuf_tensor("p3sc", [128, 64], F32))

            rfv = ridx_f.ap().rearrange("q (tt j) -> q tt j", j=2)
            rwv = rw_t.ap().rearrange("q (tt e) -> q tt e", e=8)
            cmv = cmat.ap().rearrange("q (tt e) -> q tt e", e=8)
            sc2 = scr.ap()[:, 0:32].rearrange("q (tt j) -> q tt j", j=2)
            kk.v(lambda en: en.tensor_copy(scr[:, 0:32], ridx_f[:, :]),
                 inc="V")
            kk.w("vector", "V", kk.c["V"])
            for e_ in range(NE):
                kk.v(lambda en, e_=e_: en.tensor_scalar(
                    scr[:, 0:32], ridx_f[:, :], e_ - 0.5, None, ALU.is_ge))
                kk.v(lambda en, e_=e_: en.tensor_scalar(
                    scr[:, 32:64], ridx_f[:, :], e_ + 0.5, None, ALU.is_ge),
                    inc="V")
                kk.w("vector", "V", kk.c["V"])
                kk.v(lambda en: en.tensor_sub(scr[:, 0:32], scr[:, 0:32],
                                              scr[:, 32:64]), inc="V")
                kk.w("vector", "V", kk.c["V"])
                kk.v(lambda en: en.tensor_add(sc2[:, :, 0:1], sc2[:, :, 0:1],
                                              sc2[:, :, 1:2]), inc="V")
                kk.w("vector", "V", kk.c["V"])
                kk.v(lambda en, e_=e_: en.scalar_tensor_tensor(
                    cmv[:, :, e_:e_ + 1], sc2[:, :, 0:1], mask_pp[:, 0:1],
                    rwv[:, :, e_:e_ + 1], ALU.mult, ALU.mult), inc="V")
            cm_ok = kk.c["V"]
            ctev = [0, 0]
            for tt in range(16):
                pt = (pE, pR)[tt % 2]
                kk.w("tensor", "V", cm_ok)
                kk.w("tensor", "V", ctev[tt % 2])
                kk.t(lambda e, pt=pt, tt=tt: e.transpose(
                    pt[0:8, 0:128], cmat[:, tt * 8:(tt + 1) * 8], identf[:, :]),
                    inc="T")
                mt = kk.c["T"]
                kk.w("vector", "T", mt)
                kk.v(lambda e, pt=pt, tt=tt: e.tensor_copy(
                    cTt[0:8, tt * 128:(tt + 1) * 128], pt[0:8, 0:128]), inc="V")
                ctev[tt % 2] = kk.c["V"]
            cT_ok = kk.c["V"]

            def moe_gup_load(e_, gate):
                kk.w("gpsimd", "T", gate)
                kk.g(lambda en, e_=e_: en.dma_start(
                    gup_sb.ap().rearrange("q (kc m) -> q kc m", kc=8),
                    gup_in[e_].rearrange("(kc q) m -> q kc m", q=128)),
                    inc="dW", k=16)
                return kk.c["dW"]

            def moe_down_load(e_, gate):
                kk.w("gpsimd", "T", gate)
                kk.g(lambda en, e_=e_: en.dma_start(
                    down_sb.ap().rearrange("q (kc m) -> q kc m", kc=8),
                    down_in[e_].rearrange("(kc q) m -> q kc m", q=128)),
                    inc="dW", k=16)
                return kk.c["dW"]

            gup_ld = moe_gup_load(0, 0)
            down_ld = moe_down_load(0, 0)
            glu_ev = [0, 0]
            dn_ev = [0, 0]
            for e_ in range(NE):
                kk.w("tensor", "dW", gup_ld)
                kk.w("tensor", "V", dn_ev[0])
                kk.w("tensor", "V", dn_ev[1])
                for fc in range(8):
                    for tg in range(4):
                        pg = (pA, pB)[tg % 2]
                        pu = (pC, pD)[tg % 2]
                        kk.w("tensor", "V", glu_ev[tg % 2])
                        for kc in range(8):
                            kk.t(lambda en, pg=pg, kc=kc, fc=fc, tg=tg: en.matmul(
                                pg[:, :],
                                gup_sb[:, kc * 2048 + fc * 128:
                                       kc * 2048 + (fc + 1) * 128],
                                xslot[:, kc * 2048 + tg * 512:
                                      kc * 2048 + (tg + 1) * 512],
                                start=(kc == 0), stop=(kc == 7)))
                        for kc in range(8):
                            kk.t(lambda en, pu=pu, kc=kc, fc=fc, tg=tg: en.matmul(
                                pu[:, :],
                                gup_sb[:, kc * 2048 + 1024 + fc * 128:
                                       kc * 2048 + 1024 + (fc + 1) * 128],
                                xslot[:, kc * 2048 + tg * 512:
                                      kc * 2048 + (tg + 1) * 512],
                                start=(kc == 0), stop=(kc == 7)),
                                inc="T" if kc == 7 else None)
                        mm = kk.c["T"]
                        kk.w("vector", "T", mm)
                        kk.v(lambda en, pg=pg, e_=e_, fc=fc: en.tensor_scalar(
                            gbuf[:, :], pg[:, :],
                            gub_pp[:, e_ * 16 + fc: e_ * 16 + fc + 1], 7.0,
                            ALU.add, ALU.min), inc="V")
                        g_ok = kk.c["V"]
                        kk.w("scalar", "V", g_ok)
                        kk.s(lambda en: en.activation(sgm[:, :], gbuf[:, :],
                                                      AF.Sigmoid, scale=1.702),
                             inc="S")
                        sg_ok = kk.c["S"]
                        kk.w("vector", "S", sg_ok)
                        kk.v(lambda en: en.tensor_mul(gbuf[:, :], gbuf[:, :],
                                                      sgm[:, :]))
                        kk.v(lambda en, pu=pu, e_=e_, fc=fc: en.tensor_scalar(
                            sgm[:, :], pu[:, :],
                            gub_pp[:, e_ * 16 + 8 + fc: e_ * 16 + 9 + fc], 7.0,
                            ALU.add, ALU.min))
                        kk.v(lambda en: en.tensor_scalar(
                            sgm[:, :], sgm[:, :], -7.0, 1.0, ALU.max, ALU.add))
                        kk.v(lambda en, fc=fc, tg=tg: en.tensor_mul(
                            glu_sb[:, fc * 2048 + tg * 512:
                                   fc * 2048 + (tg + 1) * 512],
                            sgm[:, :], gbuf[:, :]), inc="V")
                        glu_ev[tg % 2] = kk.c["V"]
                glu_done = kk.c["V"]
                gup_T = kk.c["T"]
                if e_ == 0 and "p3g" in dbg:
                    kk.w("gpsimd", "V", glu_done)
                    kk.g(lambda e: e.dma_start(dbg["p3g"], glu_sb[:, :]),
                         inc="dW", k=16)
                    gup_T = gup_T
                if e_ == 0 and "p3c" in dbg:
                    kk.w("sync", "V", cT_ok)
                    kk.y(lambda e: e.dma_start(dbg["p3c"][0:8, 0:2048], cTt[:, :]),
                         inc="dX", k=16)
                if e_ + 1 < NE:
                    gup_ld = moe_gup_load(e_ + 1, gup_T)

                cb_banks = (pC, pD, pE, pR)
                kk.w("tensor", "V", glu_done)
                kk.w("tensor", "V", dn_ev[0])
                kk.w("tensor", "V", dn_ev[1])
                for tg in range(4):
                    kk.t(lambda en, pt=cb_banks[tg], e_=e_, tg=tg: en.matmul(
                        pt[:, :], oh8f[0:8, e_ * 128:(e_ + 1) * 128],
                        cTt[0:8, tg * 512:(tg + 1) * 512], start=True, stop=True),
                        inc="T")
                cb_mm = kk.c["T"]

                kk.w("tensor", "dW", down_ld)
                for hc in range(8):
                    for tg in range(4):
                        i = hc * 4 + tg
                        pt = (pA, pB)[i % 2]
                        kk.w("tensor", "V", dn_ev[i % 2])
                        for ic in range(8):
                            kk.t(lambda en, pt=pt, ic=ic, hc=hc, tg=tg: en.matmul(
                                pt[:, :],
                                down_sb[:, ic * 1024 + hc * 128:
                                        ic * 1024 + (hc + 1) * 128],
                                glu_sb[:, ic * 2048 + tg * 512:
                                       ic * 2048 + (tg + 1) * 512],
                                start=(ic == 0), stop=(ic == 7)),
                                inc="T" if ic == 7 else None)
                        mm = kk.c["T"]
                        kk.w("vector", "T", mm)
                        kk.w("vector", "T", cb_mm)
                        kk.v(lambda en, pt=pt, e_=e_, hc=hc: en.tensor_scalar(
                            gbuf[:, :], pt[:, :],
                            db_pp[:, e_ * 8 + hc: e_ * 8 + hc + 1], None, ALU.add))
                        kk.v(lambda en, cb=cb_banks[tg]: en.tensor_mul(
                            gbuf[:, :], gbuf[:, :], cb[:, :]))
                        dst = accA[:, hc * 2048 + tg * 512:
                                   hc * 2048 + (tg + 1) * 512]
                        kk.v(lambda en, dst=dst: en.tensor_add(dst, dst,
                                                               gbuf[:, :]),
                             inc="V")
                        dn_ev[i % 2] = kk.c["V"]
                down_T = kk.c["T"]
                if e_ + 1 < NE:
                    down_ld = moe_down_load(e_ + 1, down_T)

            kk.barrier()
            if "p3" in dbg:
                kk.y(lambda e: e.dma_start(dbg["p3"], accA[:, :]), inc="dX", k=16)
                kk.w("sync", "dX", kk.c["dX"])
                kk.barrier()
            kk.flush()

        if STOP_AFTER not in ("p1", "p2", "p3"):
          with ExitStack() as ph:
            xt = [ph.enter_context(nc.sbuf_tensor(f"p4x{i}", [128, D], F32))
                  for i in range(2)]
            used = [0, 0]
            tev = [0, 0]
            for tt in range(16):
                b = tt % 2
                kk.w("sync", "V", used[b])
                kk.y(lambda e, b=b, tt=tt: e.dma_start(
                    xt[b][:, :], x_in[tt * 128:(tt + 1) * 128, :]), inc="dX", k=16)
                ld = kk.c["dX"]
                kk.w("tensor", "dX", ld)
                for r in range(8):
                    pt = (pA, pB)[r % 2]
                    kk.w("tensor", "V", tev[r % 2])
                    kk.t(lambda e, pt=pt, b=b, r=r: e.transpose(
                        pt[:, 0:128], xt[b][:, r * 128:(r + 1) * 128],
                        identf[:, :]), inc="T")
                    td = kk.c["T"]
                    kk.w("vector", "T", td)
                    dst = accA[:, r * 2048 + tt * 128: r * 2048 + (tt + 1) * 128]
                    kk.v(lambda e, pt=pt, dst=dst: e.tensor_add(
                        dst, dst, pt[:, 0:128]), inc="V")
                    tev[r % 2] = kk.c["V"]
                used[b] = kk.c["V"]
            kk.barrier()
            if "p4" in dbg:
                kk.y(lambda e: e.dma_start(dbg["p4"], accA[:, :]), inc="dX", k=16)
                kk.w("sync", "dX", kk.c["dX"])
                kk.barrier()
            kk.flush()

        if STOP_AFTER not in ("p1", "p2", "p3", "p4"):
          with ExitStack() as ph:
            sq2 = ph.enter_context(nc.sbuf_tensor("p5sq", [128, 2048], F32))
            rows = ph.enter_context(nc.sbuf_tensor("p5rows", [1, 4096], F32))
            mbb = ph.enter_context(nc.sbuf_tensor("p5mb", [128, 2048], F32))
            rbb = ph.enter_context(nc.sbuf_tensor("p5rb", [128, 2048], F32))
            bnc = ph.enter_context(nc.sbuf_tensor("p5bn", [128, 2048], F32))

            kk.y(lambda e: e.dma_start(scr_x1, accA[:, :]), inc="dX", k=16)
            spill = kk.c["dX"]

            def fm_stats(src):
                evS = [0, 0]
                sqev = [0, 0]
                for tg in range(4):
                    for r in range(8):
                        if r == 0:
                            kk.w("tensor", "S", evS[0])
                        kk.t(lambda e, r=r, tg=tg, src=src: e.matmul(
                            pR[0:1, :], onesf[:, 0:1],
                            src[:, r * 2048 + tg * 512: r * 2048 + (tg + 1) * 512],
                            start=(r == 0), stop=(r == 7)),
                            inc="T" if r == 7 else None)
                    mt = kk.c["T"]
                    kk.w("scalar", "T", mt)
                    kk.s(lambda e, tg=tg: e.copy(
                        rows[0:1, tg * 512:(tg + 1) * 512], pR[0:1, :]), inc="S")
                    evS[0] = kk.c["S"]
                    for r in range(8):
                        sl = sq2[:, (r % 2) * 512:(r % 2) * 512 + 512]
                        kk.w("scalar", "T", sqev[r % 2])
                        if r == 0:
                            kk.w("tensor", "S", evS[1])
                        kk.s(lambda e, r=r, tg=tg, src=src, sl=sl: e.activation(
                            sl, src[:, r * 2048 + tg * 512:
                                    r * 2048 + (tg + 1) * 512], AF.Square),
                            inc="S")
                        sv = kk.c["S"]
                        kk.w("tensor", "S", sv)
                        kk.t(lambda e, r=r, sl=sl: e.matmul(
                            pE[0:1, :], onesf[:, 0:1], sl,
                            start=(r == 0), stop=(r == 7)), inc="T")
                        sqev[r % 2] = kk.c["T"]
                    mt = kk.c["T"]
                    kk.w("scalar", "T", mt)
                    kk.s(lambda e, tg=tg: e.copy(
                        rows[0:1, 2048 + tg * 512: 2048 + (tg + 1) * 512],
                        pE[0:1, :]), inc="S")
                    evS[1] = kk.c["S"]
                sums_ok = kk.c["S"]
                kk.w("vector", "S", sums_ok)
                mr = rows[0:1, 0:2048]
                rr = rows[0:1, 2048:4096]
                kk.v(lambda e, mr=mr: e.tensor_scalar(mr, mr, 1.0 / D, None,
                                                      ALU.mult))
                kk.v(lambda e, rr=rr: e.tensor_scalar(rr, rr, 1.0 / D, None,
                                                      ALU.mult), inc="V")
                kk.w("scalar", "V", kk.c["V"])
                kk.s(lambda e, mr=mr: e.activation(sq2[0:1, 0:2048], mr,
                                                   AF.Square), inc="S")
                kk.w("vector", "S", kk.c["S"])
                kk.v(lambda e, rr=rr: e.tensor_sub(rr, rr, sq2[0:1, 0:2048]),
                     inc="V")
                kk.w("scalar", "V", kk.c["V"])
                kk.s(lambda e, rr=rr: e.activation(sq2[0:1, 0:2048], rr, AF.Sqrt,
                                                   bias=epsc[0:1, 0:1]), inc="S")
                kk.w("vector", "S", kk.c["S"])
                kk.v(lambda e, rr=rr: e.reciprocal(rr, sq2[0:1, 0:2048]), inc="V")
                return kk.c["V"]

            def bcast_rows(gate_v):
                evS = [0, 0]
                for j, dst in ((0, mbb), (1, rbb)):
                    for tg in range(4):
                        pt = (pE, pR)[tg % 2]
                        kk.w("tensor", "V", gate_v)
                        kk.w("tensor", "S", evS[tg % 2])
                        kk.t(lambda e, pt=pt, j=j, tg=tg: e.matmul(
                            pt[:, :], onesrow[0:1, 0:128],
                            rows[0:1, j * 2048 + tg * 512:
                                 j * 2048 + (tg + 1) * 512],
                            start=True, stop=True), inc="T")
                        mt = kk.c["T"]
                        kk.w("scalar", "T", mt)
                        kk.s(lambda e, pt=pt, dst=dst, tg=tg: e.copy(
                            dst[:, tg * 512:(tg + 1) * 512], pt[:, :]), inc="S")
                        evS[tg % 2] = kk.c["S"]
                return kk.c["S"]

            st1 = fm_stats(accA)
            if "p5r" in dbg:
                kk.w("sync", "V", st1)
                kk.y(lambda e: e.dma_start(dbg["p5r"][0:1, 0:4096], rows[0:1, :]),
                     inc="dX", k=16)
            b1 = bcast_rows(st1)
            kk.w("sync", "V", st1)
            kk.y(lambda e: e.dma_start(scr_rows, rows[0:1, :]), inc="dX", k=16)
            rowspill = kk.c["dX"]
            kk.w("vector", "S", b1)
            kk.w("vector", "dX", spill)
            sub_ok = 0
            for r in range(8):
                sl = accA[:, r * 2048:(r + 1) * 2048]
                kk.v(lambda e, sl=sl: e.tensor_sub(bnc[:, :], sl, mbb[:, :]))
                kk.v(lambda e, sl=sl: e.tensor_mul(sl, bnc[:, :], rbb[:, :]),
                     inc="V")
            xn2_ok = kk.c["V"]
            accB = accA

            kk.w("tensor", "V", xn2_ok)
            kk.w("scalar", "dX", rowspill)
            st2 = fm_stats(accB)
            b2 = bcast_rows(st2)
            kk.w("vector", "S", b2)
            for r in range(8):
                sl = accB[:, r * 2048:(r + 1) * 2048]
                kk.v(lambda e, sl=sl: e.tensor_sub(bnc[:, :], sl, mbb[:, :]))
                kk.v(lambda e: e.tensor_mul(bnc[:, :], bnc[:, :], rbb[:, :]),
                     inc="V")
                kk.v(lambda e, r=r: e.tensor_scalar(
                    xslot[:, r * 2048:(r + 1) * 2048], bnc[:, :],
                    m_lnw[:, r:r + 1], m_lnb[:, r:r + 1], ALU.mult, ALU.add),
                    inc="V")
            if "p5u" in dbg:
                kk.w("gpsimd", "V", kk.c["V"])
                kk.g(lambda e: e.dma_start(dbg["p5u"], xslot[:, :]), inc="dW", k=16)
            kk.barrier()
            if "p5" in dbg:
                kk.y(lambda e: e.dma_start(dbg["p5"], accB[:, :]), inc="dX", k=16)
                kk.w("sync", "dX", kk.c["dX"])
                kk.barrier()
            kk.flush()

        if STOP_AFTER not in ("p1", "p2", "p3", "p4", "p5"):
          accB = accA
          with ExitStack() as ph:
            w_ip = [ph.enter_context(nc.sbuf_tensor(f"p6wi{i}", [128, 4096], BF16))
                    for i in range(2)]
            w_op = ph.enter_context(nc.sbuf_tensor("p6wo", [128, 4096], BF16))
            w_xp = ph.enter_context(nc.sbuf_tensor("p6wx", [128, 1536], BF16))
            w_dt = ph.enter_context(nc.sbuf_tensor("p6wd", [64, 2048], BF16))
            xc_s = ph.enter_context(nc.sbuf_tensor("p6xc", [128, 16 * TC], BF16))
            dl_s = ph.enter_context(nc.sbuf_tensor("p6dl", [128, 16 * TC], F32))
            dx_s = ph.enter_context(nc.sbuf_tensor("p6dx", [128, 16 * TC], BF16))
            ya_s = [ph.enter_context(nc.sbuf_tensor(f"p6y{i}", [128, TC], F32))
                    for i in range(2)]
            yb_s = ph.enter_context(nc.sbuf_tensor("p6yb", [128, 16 * TC], BF16))
            hstate = ph.enter_context(nc.sbuf_tensor("p6hs", [128, 256], F32))
            hist = [ph.enter_context(nc.sbuf_tensor(f"p6h{i}", [128, TC], F32))
                    for i in range(2)]
            dAt = [ph.enter_context(nc.sbuf_tensor(f"p6a{i}", [128, TC], F32))
                   for i in range(2)]
            dBu = [ph.enter_context(nc.sbuf_tensor(f"p6b{i}", [128, TC], F32))
                   for i in range(2)]
            bct = ph.enter_context(nc.sbuf_tensor("p6bc", [96, TC], BF16))
            dt16 = ph.enter_context(nc.sbuf_tensor("p6dt", [64, TC], BF16))
            zsl = [ph.enter_context(nc.sbuf_tensor(f"p6z{i}", [128, TC], F32))
                   for i in range(2)]
            cvt = ph.enter_context(nc.sbuf_tensor("p6cv", [128, TC], F32))
            slv = ph.enter_context(nc.sbuf_tensor("p6sl", [128, 64], F32))

            kk.g(lambda e: e.dma_start(
                w_xp.ap().rearrange("q (kc m) -> q kc m", kc=16),
                mxp_in.rearrange("(kc q) m -> q kc m", q=128)), inc="dW", k=16)
            kk.g(lambda e: e.dma_start(w_dt[:, :], mdt_in), inc="dW", k=16)
            smallw = kk.c["dW"]

            kk.v(lambda e: e.memset(hstate[:, :], 0.0))
            kk.v(lambda e: e.memset(slv[:, :], 0.0), inc="V")

            def ld_ip(piece, buf, zhalf, gate):
                m0 = zhalf * 2048 + piece * 512
                kk.w("gpsimd", "T", gate)
                kk.g(lambda e, m0=m0, buf=buf: e.dma_start(
                    w_ip[buf].ap().rearrange("q (kc m) -> q kc m", kc=8),
                    minp_in.rearrange("(kc q) m -> q kc m", q=128)
                    [:, :, m0:m0 + 512]), inc="dW", k=16)
                return kk.c["dW"]

            def ld_op(piece, gate):
                m0 = piece * 256
                kk.w("gpsimd", "T", gate)
                kk.g(lambda e, m0=m0: e.dma_start(
                    w_op.ap().rearrange("q (kc m) -> q kc m", kc=16),
                    mout_in.rearrange("(kc q) m -> q kc m", q=128)
                    [:, :, m0:m0 + 256]), inc="dW", k=16)
                return kk.c["dW"]

            ipld = [ld_ip(0, 0, 0, 0), 0]
            xz_ev = [0, 0]
            dl_ev = [0, 0]
            z_evS = [0, 0]
            zuse = [0, 0]
            opj_ev = [0, 0]
            scan_mS = [0, 0]
            bcused = [0, 0]
            lastT = 0
            dbc_scalar_gate = 0
            silu_m = 0
            dl_add = 0

            for c in range(NCH):
                c0 = c * TC
                for piece in range(4):
                    buf = piece % 2
                    if piece + 1 < 4:
                        ipld[(piece + 1) % 2] = ld_ip(piece + 1, (piece + 1) % 2,
                                                      0, lastT)
                    kk.w("tensor", "dW", ipld[buf])
                    for sub in range(4):
                        dt_i = piece * 4 + sub
                        pt = (pA, pB)[dt_i % 2]
                        kk.w("tensor", "V", xz_ev[dt_i % 2])
                        for kc in range(8):
                            kk.t(lambda e, pt=pt, kc=kc, sub=sub, c0=c0, buf=buf:
                                 e.matmul(
                                     pt[:, 0:TC],
                                     w_ip[buf][:, kc * 512 + sub * 128:
                                               kc * 512 + (sub + 1) * 128],
                                     xslot[:, kc * 2048 + c0:
                                           kc * 2048 + c0 + TC],
                                     start=(kc == 0), stop=(kc == 7)),
                                 inc="T" if kc == 7 else None)
                        lastT = kk.c["T"]
                        mm = kk.c["T"]
                        kk.w("vector", "T", mm)
                        kk.w("vector", "S", silu_m)
                        kk.v(lambda e, pt=pt, dt_i=dt_i: e.tensor_scalar(
                            cvt[:, :], pt[:, 0:TC],
                            m_cw[:, dt_i * 4 + 3: dt_i * 4 + 4], None, ALU.mult))
                        for kcv in range(3):
                            sh = 3 - kcv
                            kk.v(lambda e, pt=pt, dt_i=dt_i, kcv=kcv, sh=sh:
                                 e.scalar_tensor_tensor(
                                     cvt[:, sh:TC], pt[:, 0:TC - sh],
                                     m_cw[:, dt_i * 4 + kcv: dt_i * 4 + kcv + 1],
                                     cvt[:, sh:TC], ALU.mult, ALU.add))
                            kk.v(lambda e, dt_i=dt_i, kcv=kcv, sh=sh:
                                 e.scalar_tensor_tensor(
                                     cvt[:, 0:sh],
                                     slv[:, dt_i * 4 + kcv: dt_i * 4 + kcv + sh],
                                     m_cw[:, dt_i * 4 + kcv: dt_i * 4 + kcv + 1],
                                     cvt[:, 0:sh], ALU.mult, ALU.add))
                        kk.v(lambda e, pt=pt, dt_i=dt_i: e.tensor_copy(
                            slv[:, dt_i * 4: dt_i * 4 + 3], pt[:, TC - 3:TC]),
                            inc="V")
                        xz_ev[dt_i % 2] = kk.c["V"]
                        cv_ok = kk.c["V"]
                        kk.w("scalar", "V", cv_ok)
                        kk.s(lambda e, dt_i=dt_i: e.activation(
                            xc_s[:, dt_i * TC:(dt_i + 1) * TC], cvt[:, :],
                            AF.Silu, bias=m_cb[:, dt_i:dt_i + 1]), inc="S")
                        silu_m = kk.c["S"]
                xc_ok = kk.c["S"]

                if c == 0 and "p6x" in dbg:
                    kk.w("gpsimd", "S", xc_ok)
                    kk.g(lambda e: e.dma_start(dbg["p6x"][:, 0:16 * TC],
                                               xc_s[:, :]), inc="dW", k=16)
                kk.w("tensor", "S", xc_ok)
                kk.w("tensor", "dW", smallw)
                for kc in range(16):
                    kk.t(lambda e, kc=kc: e.matmul(
                        pC[0:96, 0:TC], w_xp[:, kc * 96:(kc + 1) * 96],
                        xc_s[:, kc * TC:(kc + 1) * TC],
                        start=(kc == 0), stop=(kc == 15)),
                        inc="T" if kc == 15 else None)
                dbc_mm = kk.c["T"]
                kk.w("scalar", "T", dbc_mm)
                kk.w("scalar", "T", dbc_scalar_gate)
                kk.s(lambda e: e.copy(dt16[:, :], pC[0:64, 0:TC]))
                kk.s(lambda e: e.copy(bct[64:96, :], pC[64:96, 0:TC]), inc="S")
                dbc_ev = kk.c["S"]

                kk.w("tensor", "S", dbc_ev)
                kk.w("scalar", "V", zuse[0])
                for dt_i in range(16):
                    pt = (pA, pB)[dt_i % 2]
                    kk.w("tensor", "V", dl_ev[dt_i % 2])
                    kk.t(lambda e, pt=pt, dt_i=dt_i: e.matmul(
                        pt[:, 0:TC], w_dt[:, dt_i * 128:(dt_i + 1) * 128],
                        dt16[:, :], start=True, stop=True), inc="T")
                    mm = kk.c["T"]
                    kk.w("scalar", "T", mm)
                    kk.w("scalar", "V", dl_add)
                    kk.s(lambda e, pt=pt, dt_i=dt_i: e.activation(
                        cvt[:, :], pt[:, 0:TC], AF.Abs,
                        bias=m_dtb[:, dt_i:dt_i + 1]))
                    kk.s(lambda e: e.activation(cvt[:, :], cvt[:, :], AF.Exp,
                                                scale=-1.0))
                    kk.s(lambda e: e.activation(zsl[0][:, :], cvt[:, :], AF.Ln,
                                                bias=1.0))
                    kk.s(lambda e, pt=pt, dt_i=dt_i: e.activation(
                        cvt[:, :], pt[:, 0:TC], AF.Relu,
                        bias=m_dtb[:, dt_i:dt_i + 1]), inc="S")
                    sp = kk.c["S"]
                    kk.w("vector", "S", sp)
                    kk.v(lambda e, dt_i=dt_i: e.tensor_add(
                        dl_s[:, dt_i * TC:(dt_i + 1) * TC], cvt[:, :],
                        zsl[0][:, :]), inc="V")
                    dl_ev[dt_i % 2] = kk.c["V"]
                    dl_add = kk.c["V"]

                if c == 0 and "p6d" in dbg:
                    kk.w("sync", "V", kk.c["V"])
                    kk.y(lambda e: e.dma_start(dbg["p6d"][:, 0:16 * TC],
                                               dl_s[:, :]), inc="dX", k=16)
                for dt_i in range(16):
                    kk.v(lambda e, dt_i=dt_i: e.tensor_mul(
                        dx_s[:, dt_i * TC:(dt_i + 1) * TC],
                        dl_s[:, dt_i * TC:(dt_i + 1) * TC],
                        xc_s[:, dt_i * TC:(dt_i + 1) * TC]),
                        inc="V" if dt_i == 15 else None)
                dx_ok = kk.c["V"]

                opl = ld_op(0, lastT)
                zpld = [0, 0]
                for grp in range(4):
                    zpld[grp % 2] = ld_ip(grp, grp % 2, 1, lastT)
                    for sub in range(4):
                        dt_i = grp * 4 + sub
                        ya = ya_s[dt_i % 2]
                        for n in range(NST):
                            pb_, pc_ = ((pE, pR), (pC, pD))[n % 2]
                            kk.w("tensor", "S", dbc_ev)
                            kk.w("tensor", "V", bcused[n % 2])
                            if n == 0:
                                kk.w("tensor", "S", z_evS[0])
                                kk.w("tensor", "S", z_evS[1])
                            kk.t(lambda e, pb_=pb_, n=n: e.matmul(
                                pb_[:, 0:TC],
                                oh32b[64:96, n * 128:(n + 1) * 128],
                                bct[64:96, :], start=True, stop=True))
                            kk.t(lambda e, pc_=pc_, n=n: e.matmul(
                                pc_[:, 0:TC],
                                oh32b[64:96, (16 + n) * 128:(17 + n) * 128],
                                bct[64:96, :], start=True, stop=True), inc="T")
                            bc_mm = kk.c["T"]
                            hb = n % 2
                            kk.w("scalar", "V", scan_mS[hb])
                            kk.s(lambda e, hb=hb, dt_i=dt_i, n=n: e.activation(
                                dAt[hb][:, :],
                                dl_s[:, dt_i * TC:(dt_i + 1) * TC], AF.Exp,
                                scale=m_app[:, dt_i * 16 + n:
                                            dt_i * 16 + n + 1]), inc="S")
                            dA_ok = kk.c["S"]
                            kk.w("vector", "T", bc_mm)
                            if n == 0 and sub == 0 and grp == 0:
                                kk.w("vector", "V", dx_ok)
                            kk.v(lambda e, hb=hb, dt_i=dt_i, pb_=pb_:
                                 e.tensor_mul(
                                     dBu[hb][:, :],
                                     dx_s[:, dt_i * TC:(dt_i + 1) * TC],
                                     pb_[:, 0:TC]))
                            kk.w("vector", "S", dA_ok)
                            kk.v(lambda e, hb=hb: e.tensor_tensor_scan(
                                hist[hb][:, :], dAt[hb][:, :], dBu[hb][:, :],
                                hstate[:, dt_i * 16 + n: dt_i * 16 + n + 1]
                                if False else
                                hstate[:, dt_i * 16 + n: dt_i * 16 + n + 1],
                                ALU.mult, ALU.add))
                            kk.v(lambda e, hb=hb, pc_=pc_: e.tensor_mul(
                                dBu[hb][:, :], hist[hb][:, :], pc_[:, 0:TC]))
                            kk.v(lambda e, hb=hb, dt_i=dt_i, n=n: e.tensor_copy(
                                hstate[:, dt_i * 16 + n: dt_i * 16 + n + 1],
                                hist[hb][:, TC - 1:TC]))
                            if n == 0:
                                kk.v(lambda e, hb=hb, ya=ya: e.tensor_copy(
                                    ya[:, :], dBu[hb][:, :]), inc="V")
                            else:
                                kk.v(lambda e, hb=hb, ya=ya: e.tensor_add(
                                    ya[:, :], ya[:, :], dBu[hb][:, :]), inc="V")
                            scan_mS[hb] = kk.c["V"]
                            bcused[n % 2] = kk.c["V"]
                        kk.v(lambda e, dt_i=dt_i, ya=ya: e.scalar_tensor_tensor(
                            ya[:, :], xc_s[:, dt_i * TC:(dt_i + 1) * TC],
                            m_dpp[:, dt_i:dt_i + 1], ya[:, :],
                            ALU.mult, ALU.add), inc="V")
                        yD_ok = kk.c["V"]
                        zb = dt_i % 2
                        kk.w("tensor", "dW", zpld[grp % 2])
                        kk.w("tensor", "S", z_evS[zb])
                        kk.w("tensor", "V", bcused[0])
                        for kc in range(8):
                            kk.t(lambda e, kc=kc, sub=sub, c0=c0, g2=grp % 2:
                                 e.matmul(
                                     pE[:, 0:TC],
                                     w_ip[g2][:, kc * 512 + sub * 128:
                                              kc * 512 + (sub + 1) * 128],
                                     xslot[:, kc * 2048 + c0:
                                           kc * 2048 + c0 + TC],
                                     start=(kc == 0), stop=(kc == 7)),
                                 inc="T" if kc == 7 else None)
                        lastT = kk.c["T"]
                        zmm = kk.c["T"]
                        kk.w("scalar", "T", zmm)
                        kk.w("scalar", "V", yD_ok)
                        kk.w("scalar", "V", zuse[zb])
                        kk.s(lambda e, zb=zb: e.activation(
                            zsl[zb][:, :], pE[:, 0:TC], AF.Silu), inc="S")
                        z_evS[zb] = kk.c["S"]
                        kk.w("vector", "S", z_evS[zb])
                        kk.v(lambda e, dt_i=dt_i, zb=zb, ya=ya: e.tensor_mul(
                            yb_s[:, dt_i * TC:(dt_i + 1) * TC], ya[:, :],
                            zsl[zb][:, :]), inc="V")
                        zuse[zb] = kk.c["V"]
                yb_ok = kk.c["V"]

                if c == 0 and "p6y" in dbg:
                    kk.w("gpsimd", "V", yb_ok)
                    kk.g(lambda e: e.dma_start(dbg["p6y"][:, 0:16 * TC],
                                               yb_s[:, :]), inc="dW", k=16)
                for piece in range(4):
                    if piece > 0:
                        opl = ld_op(piece, lastT)
                    kk.w("tensor", "dW", opl)
                    kk.w("tensor", "V", yb_ok)
                    for sub in range(2):
                        mc = piece * 2 + sub
                        pt = (pA, pB)[mc % 2]
                        kk.w("tensor", "V", opj_ev[mc % 2])
                        for kc in range(16):
                            kk.t(lambda e, pt=pt, kc=kc, sub=sub: e.matmul(
                                pt[:, 0:TC],
                                w_op[:, kc * 256 + sub * 128:
                                     kc * 256 + (sub + 1) * 128],
                                yb_s[:, kc * TC:(kc + 1) * TC],
                                start=(kc == 0), stop=(kc == 15)),
                                inc="T" if kc == 15 else None)
                        lastT = kk.c["T"]
                        mm = kk.c["T"]
                        kk.w("vector", "T", mm)
                        dst = accB[:, mc * 2048 + c0: mc * 2048 + c0 + TC]
                        kk.v(lambda e, pt=pt, dst=dst: e.tensor_add(
                            dst, dst, pt[:, 0:TC]), inc="V")
                        opj_ev[mc % 2] = kk.c["V"]
                dbc_scalar_gate = lastT
                if c + 1 < NCH:
                    ipld[0] = ld_ip(0, 0, 0, lastT)

            kk.barrier()
            if "p6" in dbg:
                kk.y(lambda e: e.dma_start(dbg["p6"], accB[:, :]), inc="dX", k=16)
                kk.w("sync", "dX", kk.c["dX"])
                kk.barrier()
            kk.flush()

        if STOP_AFTER not in ("p1", "p2", "p3", "p4", "p5", "p6"):
          accB = accA
          with ExitStack() as ph:
            stg = [ph.enter_context(nc.sbuf_tensor(f"p7s{i}", [128, 2048], F32))
                   for i in range(2)]
            rows = ph.enter_context(nc.sbuf_tensor("p7rw", [1, 4096], F32))
            mbb = ph.enter_context(nc.sbuf_tensor("p7mb", [128, 2048], F32))
            rbb = ph.enter_context(nc.sbuf_tensor("p7rb", [128, 2048], F32))
            bnc = ph.enter_context(nc.sbuf_tensor("p7bn", [128, 2048], F32))
            kk.y(lambda e: e.dma_start(rows[0:1, :], scr_rows), inc="dX", k=16)
            rld = kk.c["dX"]
            kk.w("tensor", "dX", rld)
            evS = [0, 0]
            for j, dst in ((0, mbb), (1, rbb)):
                for tg in range(4):
                    pt = (pE, pR)[tg % 2]
                    kk.w("tensor", "S", evS[tg % 2])
                    kk.t(lambda e, pt=pt, j=j, tg=tg: e.matmul(
                        pt[:, :], onesrow[0:1, 0:128],
                        rows[0:1, j * 2048 + tg * 512: j * 2048 + (tg + 1) * 512],
                        start=True, stop=True), inc="T")
                    mt = kk.c["T"]
                    kk.w("scalar", "T", mt)
                    kk.s(lambda e, pt=pt, dst=dst, tg=tg: e.copy(
                        dst[:, tg * 512:(tg + 1) * 512], pt[:, :]), inc="S")
                    evS[tg % 2] = kk.c["S"]
            bc_ok = kk.c["S"]
            kk.w("vector", "S", bc_ok)
            used = [0, 0]
            bnc_free = 0
            for r in range(8):
                b = r % 2
                kk.w("sync", "S", used[b])
                kk.y(lambda e, b=b, r=r: e.dma_start(
                    stg[b][:, :], scr_x1[:, r * 2048:(r + 1) * 2048]),
                    inc="dX", k=16)
                ld = kk.c["dX"]
                kk.w("vector", "dX", ld)
                kk.w("vector", "S", bnc_free)
                kk.v(lambda e, b=b: e.tensor_sub(bnc[:, :], stg[b][:, :],
                                                 mbb[:, :]))
                kk.v(lambda e, b=b: e.tensor_mul(bnc[:, :], bnc[:, :],
                                                 rbb[:, :]), inc="V")
                sv = kk.c["V"]
                kk.w("scalar", "V", sv)
                kk.s(lambda e, r=r: e.copy(
                    xslot[:, r * 2048:(r + 1) * 2048], bnc[:, :]), inc="S")
                used[b] = kk.c["S"]
                bnc_free = kk.c["S"]
            kk.barrier()
            kk.flush()

          with ExitStack() as ph:
            z_sb = ph.enter_context(nc.sbuf_tensor("p7z", [128, 16384], BF16))
            s_sb = ph.enter_context(nc.sbuf_tensor("p7ss", [128, 16384], BF16))
            wbuf = [ph.enter_context(nc.sbuf_tensor(f"p7w{i}", [128, 4096], BF16))
                    for i in range(2)]
            weT = ph.enter_context(nc.sbuf_tensor("p7weT", [8, 2048], F32))
            lg = ph.enter_context(nc.sbuf_tensor("p7lg", [128, 128], F32))
            lgs = ph.enter_context(nc.sbuf_tensor("p7lgs", [128, 128], F32))
            wgw = ph.enter_context(nc.sbuf_tensor("p7gw", [128, 64], BF16))
            tp1 = ph.enter_context(nc.sbuf_tensor("p7t1", [128, 1024], F32))
            tp2 = ph.enter_context(nc.sbuf_tensor("p7t2", [128, 64], F32))
            sml = ph.enter_context(nc.sbuf_tensor("p7sm", [128, 8], F32))

            kk.g(lambda e: e.dma_start(
                wgw.ap().rearrange("q (kc m) -> q kc m", kc=8),
                fgw_in.rearrange("(kc q) m -> q kc m", q=128)), inc="dW", k=16)
            gw_ld = kk.c["dW"]
            kk.w("tensor", "dW", gw_ld)
            lg_ev = [0, 0]
            for tt in range(16):
                pt = (pA, pB)[tt % 2]
                kk.w("tensor", "V", lg_ev[tt % 2])
                for kc in range(8):
                    kk.t(lambda e, pt=pt, kc=kc, tt=tt: e.matmul(
                        pt[:, 0:8],
                        xslot[:, kc * 2048 + tt * 128: kc * 2048 + (tt + 1) * 128],
                        wgw[:, kc * 8:(kc + 1) * 8],
                        start=(kc == 0), stop=(kc == 7)),
                        inc="T" if kc == 7 else None)
                mm = kk.c["T"]
                kk.w("vector", "T", mm)
                kk.v(lambda e, pt=pt, tt=tt: e.tensor_add(
                    lg[:, tt * 8:(tt + 1) * 8], pt[:, 0:8], f_gb[:, :]), inc="V")
                lg_ev[tt % 2] = kk.c["V"]
            for tt in range(16):
                sl = lg[:, tt * 8:(tt + 1) * 8]
                ol = lgs[:, tt * 8:(tt + 1) * 8]
                kk.v(lambda e, sl=sl: e.reduce_max(sml[:, 0:1], sl, AX.X),
                     inc="V")
                kk.w("vector", "V", kk.c["V"])
                kk.v(lambda e, sl=sl: e.tensor_scalar(
                    tp1[:, 0:8], sl, sml[:, 0:1], None, ALU.is_ge), inc="V")
                kk.w("vector", "V", kk.c["V"])
                kk.v(lambda e, sl=sl: e.scalar_tensor_tensor(
                    tp2[:, 0:8], tp1[:, 0:8], -1e30, sl, ALU.mult, ALU.add),
                    inc="V")
                kk.w("vector", "V", kk.c["V"])
                kk.v(lambda e: e.reduce_max(sml[:, 1:2], tp2[:, 0:8], AX.X),
                     inc="V")
                kk.w("vector", "V", kk.c["V"])
                kk.v(lambda e: e.tensor_scalar(
                    tp2[:, 0:8], tp2[:, 0:8], sml[:, 1:2], None, ALU.is_ge),
                    inc="V")
                kk.w("vector", "V", kk.c["V"])
                kk.v(lambda e: e.tensor_sub(sml[:, 2:3], sml[:, 0:1],
                                            sml[:, 1:2]), inc="V")
                dv = kk.c["V"]
                kk.w("scalar", "V", dv)
                kk.s(lambda e: e.activation(sml[:, 3:4], sml[:, 2:3], AF.Sigmoid),
                     inc="S")
                p1_ok = kk.c["S"]
                kk.w("vector", "S", p1_ok)
                kk.v(lambda e: e.tensor_scalar(sml[:, 4:5], sml[:, 3:4], -1.0,
                                               1.0, ALU.mult, ALU.add))
                kk.v(lambda e: e.tensor_scalar(
                    tp1[:, 0:8], tp1[:, 0:8], sml[:, 3:4], None, ALU.mult),
                    inc="V")
                kk.w("vector", "V", kk.c["V"])
                kk.v(lambda e, ol=ol: e.scalar_tensor_tensor(
                    ol, tp2[:, 0:8], sml[:, 4:5], tp1[:, 0:8], ALU.mult,
                    ALU.add), inc="V")
            we_ok = kk.c["V"]
            wtev = [0, 0]
            for tt in range(16):
                pt = (pE, pR)[tt % 2]
                kk.w("tensor", "V", we_ok)
                kk.w("tensor", "V", wtev[tt % 2])
                kk.t(lambda e, pt=pt, tt=tt: e.transpose(
                    pt[0:8, 0:128], lgs[:, tt * 8:(tt + 1) * 8], identf[:, :]),
                    inc="T")
                mt = kk.c["T"]
                kk.w("vector", "T", mt)
                kk.v(lambda e, pt=pt, tt=tt: e.tensor_copy(
                    weT[0:8, tt * 128:(tt + 1) * 128], pt[0:8, 0:128]), inc="V")
                wtev[tt % 2] = kk.c["V"]
            weT_ok = kk.c["V"]

            def ld_piece(src3, e_, piece, kcn, buf, gate):
                kk.w("gpsimd", "T", gate)
                kk.g(lambda en, src3=src3, e_=e_, piece=piece, kcn=kcn, buf=buf:
                     en.dma_start(
                         wbuf[buf].ap().rearrange("q (kc m) -> q kc m", kc=16)
                         [:, 0:kcn, :],
                         src3[e_].rearrange("(kc q) m -> q kc m", q=128)
                         [:, :, piece * 256:(piece + 1) * 256]), inc="dW", k=16)
                return kk.c["dW"]

            wld = [ld_piece(fw1_in, 0, 0, 8, 0, 0), 0]
            x1_ev = [weT_ok, weT_ok]
            z_ev = [0, 0]
            s_ev = [0, 0]
            w3_ev = [0, 0]
            w3_all = 0
            lastT = 0
            for e_ in range(NE):
                for half in range(2):
                    h0 = half * 1024
                    kk.w("tensor", "S", s_ev[0])
                    kk.w("tensor", "S", s_ev[1])
                    kk.w("tensor", "V", w3_all)
                    for fc in range(8):
                        b1b = 0
                        b2b = 1
                        if not (e_ == 0 and half == 0 and fc == 0):
                            wld[b1b] = ld_piece(fw1_in, e_, fc, 8, b1b, lastT)
                        wld[b2b] = ld_piece(fw2_in, e_, fc, 8, b2b, lastT)
                        for sub in range(2):
                            fci = fc * 2 + sub
                            kk.w("tensor", "dW", wld[b1b])
                            kk.w("tensor", "V", x1_ev[0])
                            kk.w("tensor", "V", x1_ev[1])
                            for kc in range(8):
                                for tg in range(2):
                                    kk.t(lambda en, pt=(pA, pB)[tg], kc=kc,
                                         sub=sub, h0=h0, tg=tg, b1b=b1b: en.matmul(
                                             pt[:, :],
                                             wbuf[b1b][:, kc * 256 + sub * 128:
                                                       kc * 256 + (sub + 1) * 128],
                                             xslot[:, kc * 2048 + h0 + tg * 512:
                                                   kc * 2048 + h0 + (tg + 1) * 512],
                                             start=(kc == 0), stop=(kc == 7)),
                                         inc="T" if kc == 7 else None)
                            x1mm = kk.c["T"]
                            kk.w("tensor", "dW", wld[b2b])
                            kk.w("tensor", "V", z_ev[0])
                            kk.w("tensor", "V", z_ev[1])
                            for kc in range(8):
                                for tg in range(2):
                                    kk.t(lambda en, pt2=(pC, pD)[tg], kc=kc,
                                         sub=sub, h0=h0, tg=tg, b2b=b2b: en.matmul(
                                             pt2[:, :],
                                             wbuf[b2b][:, kc * 256 + sub * 128:
                                                       kc * 256 + (sub + 1) * 128],
                                             xslot[:, kc * 2048 + h0 + tg * 512:
                                                   kc * 2048 + h0 + (tg + 1) * 512],
                                             start=(kc == 0), stop=(kc == 7)),
                                         inc="T" if kc == 7 else None)
                            lastT = kk.c["T"]
                            x2mm = kk.c["T"]
                            kk.w("vector", "T", x1mm)
                            for tg in range(2):
                                kk.v(lambda en, pt=(pA, pB)[tg], e_=e_, fci=fci,
                                     tg=tg: en.tensor_scalar(
                                         tp1[:, tg * 512:(tg + 1) * 512], pt[:, :],
                                         f_b1[:, e_ * 16 + fci: e_ * 16 + fci + 1],
                                         None, ALU.add),
                                     inc="V" if tg == 1 else None)
                            x1_ev[0] = x1_ev[1] = kk.c["V"]
                            kk.w("vector", "T", x2mm)
                            for tg in range(2):
                                kk.v(lambda en, pt2=(pC, pD)[tg], e_=e_, fci=fci,
                                     tg=tg: en.scalar_tensor_tensor(
                                         z_sb[:, fci * 1024 + tg * 512:
                                              fci * 1024 + (tg + 1) * 512],
                                         pt2[:, :],
                                         f_b2[:, e_ * 16 + fci: e_ * 16 + fci + 1],
                                         tp1[:, tg * 512:(tg + 1) * 512],
                                         ALU.add, ALU.mult),
                                     inc="V" if tg == 1 else None)
                            z_ev[0] = z_ev[1] = kk.c["V"]
                    z_ok = kk.c["V"]
                    for gc in range(8):
                        gb_ = gc % 2
                        wld[gb_] = ld_piece(fwg_in, e_, gc, 16, gb_, lastT)
                        kk.w("tensor", "dW", wld[gb_])
                        kk.w("tensor", "V", z_ok)
                        for sub in range(2):
                            gci = gc * 2 + sub
                            kk.w("tensor", "S", s_ev[0])
                            kk.w("tensor", "S", s_ev[1])
                            for kc in range(16):
                                for tg in range(2):
                                    kk.t(lambda en, pt=(pA, pB)[tg], kc=kc,
                                         sub=sub, tg=tg, gb_=gb_: en.matmul(
                                             pt[:, :],
                                             wbuf[gb_][:, kc * 256 + sub * 128:
                                                       kc * 256 + (sub + 1) * 128],
                                             z_sb[:, kc * 1024 + tg * 512:
                                                  kc * 1024 + (tg + 1) * 512],
                                             start=(kc == 0), stop=(kc == 15)),
                                         inc="T" if kc == 15 else None)
                            lastT = kk.c["T"]
                            smm = kk.c["T"]
                            kk.w("scalar", "T", smm)
                            for tg in range(2):
                                kk.s(lambda en, pt=(pA, pB)[tg], e_=e_, gci=gci,
                                     tg=tg: en.activation(
                                         s_sb[:, gci * 1024 + tg * 512:
                                              gci * 1024 + (tg + 1) * 512],
                                         pt[:, :], AF.Silu,
                                         bias=f_bg[:, e_ * 16 + gci:
                                                   e_ * 16 + gci + 1]), inc="S")
                            s_ev[0] = s_ev[1] = kk.c["S"]
                    s_ok = kk.c["S"]
                    kk.w("tensor", "V", weT_ok)
                    kk.w("tensor", "V", w3_all)
                    for tg in range(2):
                        kk.t(lambda en, pt=(pE, pR)[tg], e_=e_, tg=tg, h0=h0:
                             en.matmul(
                                 pt[:, :], oh8f[0:8, e_ * 128:(e_ + 1) * 128],
                                 weT[0:8, h0 + tg * 512: h0 + (tg + 1) * 512],
                                 start=True, stop=True), inc="T")
                    web_mm = kk.c["T"]
                    for piece in range(4):
                        pb_ = piece % 2
                        wld[pb_] = ld_piece(fw3_in, e_, piece, 16, pb_, lastT)
                        kk.w("tensor", "dW", wld[pb_])
                        kk.w("tensor", "S", s_ok)
                        for sub in range(2):
                            hc = piece * 2 + sub
                            kk.w("tensor", "V", w3_ev[0])
                            kk.w("tensor", "V", w3_ev[1])
                            for kc in range(16):
                                for tg in range(2):
                                    kk.t(lambda en, pt=(pC, pD)[tg], kc=kc,
                                         sub=sub, tg=tg, pb_=pb_: en.matmul(
                                             pt[:, :],
                                             wbuf[pb_][:, kc * 256 + sub * 128:
                                                       kc * 256 + (sub + 1) * 128],
                                             s_sb[:, kc * 1024 + tg * 512:
                                                  kc * 1024 + (tg + 1) * 512],
                                             start=(kc == 0), stop=(kc == 15)),
                                         inc="T" if kc == 15 else None)
                            lastT = kk.c["T"]
                            ymm = kk.c["T"]
                            kk.w("vector", "T", ymm)
                            kk.w("vector", "T", web_mm)
                            for tg in range(2):
                                kk.v(lambda en, pt=(pC, pD)[tg], e_=e_, hc=hc,
                                     tg=tg: en.tensor_scalar(
                                         tp1[:, tg * 512:(tg + 1) * 512],
                                         pt[:, :],
                                         f_b3[:, e_ * 8 + hc: e_ * 8 + hc + 1],
                                         None, ALU.add))
                                kk.v(lambda en, wb=(pE, pR)[tg], tg=tg:
                                     en.tensor_mul(
                                         tp1[:, tg * 512:(tg + 1) * 512],
                                         tp1[:, tg * 512:(tg + 1) * 512],
                                         wb[:, :]))
                                dst = accB[:, hc * 2048 + h0 + tg * 512:
                                           hc * 2048 + h0 + (tg + 1) * 512]
                                kk.v(lambda en, dst=dst, tg=tg: en.tensor_add(
                                    dst, dst, tp1[:, tg * 512:(tg + 1) * 512]),
                                    inc="V" if tg == 1 else None)
                            w3_ev[0] = w3_ev[1] = kk.c["V"]
                    w3_all = kk.c["V"]
                    if not (e_ == NE - 1 and half == 1):
                        ne_ = e_ + (1 if half == 1 else 0)
                        wld[0] = ld_piece(fw1_in, ne_, 0, 8, 0, lastT)

            kk.barrier()
            if "p7" in dbg:
                kk.y(lambda e: e.dma_start(dbg["p7"], accB[:, :]), inc="dX", k=16)
                kk.w("sync", "dX", kk.c["dX"])
                kk.barrier()
            kk.flush()

        with ExitStack() as ph:
            stg = [ph.enter_context(nc.sbuf_tensor(f"p8s{i}", [128, 2048], F32))
                   for i in range(2)]
            otile = [ph.enter_context(nc.sbuf_tensor(f"p8o{i}", [128, D], F32))
                     for i in range(2)]
            accB = accA
            if STOP_AFTER is None:
                used = [0, 0]
                for r in range(8):
                    b = r % 2
                    kk.w("sync", "V", used[b])
                    kk.y(lambda e, b=b, r=r: e.dma_start(
                        stg[b][:, :], scr_x1[:, r * 2048:(r + 1) * 2048]),
                        inc="dX", k=16)
                    ld = kk.c["dX"]
                    kk.w("vector", "dX", ld)
                    dst = accB[:, r * 2048:(r + 1) * 2048]
                    kk.v(lambda e, b=b, dst=dst: e.tensor_add(
                        dst, dst, stg[b][:, :]), inc="V")
                    used[b] = kk.c["V"]
                sum_ok = kk.c["V"]
            else:
                sum_ok = 0
            tev = [0, 0]
            ost = [0, 0]
            for tt in range(16):
                ob = tt % 2
                kk.w("vector", "dX", ost[ob])
                for r in range(8):
                    pt = (pA, pB)[r % 2]
                    kk.w("tensor", "V", sum_ok)
                    kk.w("tensor", "V", tev[r % 2])
                    kk.t(lambda e, pt=pt, r=r, tt=tt: e.transpose(
                        pt[:, 0:128],
                        accB[:, r * 2048 + tt * 128: r * 2048 + (tt + 1) * 128],
                        identf[:, :]), inc="T")
                    td = kk.c["T"]
                    kk.w("vector", "T", td)
                    kk.v(lambda e, pt=pt, r=r, ob=ob: e.tensor_copy(
                        otile[ob][:, r * 128:(r + 1) * 128], pt[:, 0:128]),
                        inc="V")
                    tev[r % 2] = kk.c["V"]
                row_ok = kk.c["V"]
                kk.w("sync", "V", row_ok)
                kk.y(lambda e, ob=ob, tt=tt: e.dma_start(
                    out_d[tt * 128:(tt + 1) * 128, :], otile[ob][:, :]),
                    inc="dX", k=16)
                ost[ob] = kk.c["dX"]
            kk.w("sync", "dX", kk.c["dX"])
            kk.flush()

    return nc



_NC_CACHE = {}


def _shared_inputs(i):
    f32 = np.float32

    def cc(a):
        return np.ascontiguousarray(np.asarray(a), dtype=f32)

    def pp(a, nchunk):
        return cc(np.asarray(a).reshape(nchunk, 128).T)

    import ml_dtypes

    sh = {}
    sh["ident"] = np.eye(128, dtype=f32)
    oh32 = np.zeros((96, 32 * 128), dtype=f32)
    for r in range(32):
        oh32[64 + r, r * 128:(r + 1) * 128] = 1.0
    sh["oh32f"] = oh32
    oh8 = np.zeros((8, 8 * 128), dtype=f32)
    for r in range(8):
        oh8[r, r * 128:(r + 1) * 128] = 1.0
    sh["oh8f"] = oh8

    sh["l0_wq"] = cc(i["L0_attn_Wq"])
    sh["l0_wk"] = cc(i["L0_attn_Wk"])
    sh["l0_wv"] = cc(i["L0_attn_Wv"])
    sh["l0_wo"] = cc(i["L0_attn_Wo"])
    sh["l0_bo_pp"] = pp(i["L0_attn_bo"], 8)

    gup = np.asarray(i["L0_ffn_gate_up_proj"])
    perm = np.concatenate([np.arange(0, 2048, 2), np.arange(1, 2048, 2)])
    sh["moe_gup"] = cc(gup[:, :, perm])
    gub = np.asarray(i["L0_ffn_gate_up_proj_bias"])[:, perm]
    sh["moe_gub_pp"] = cc(np.stack([gub[e].reshape(16, 128).T for e in range(8)],
                                   axis=1).reshape(128, 8 * 16))
    sh["moe_down"] = cc(i["L0_ffn_down_proj"])
    dbn = np.asarray(i["L0_ffn_down_proj_bias"])
    sh["moe_db_pp"] = cc(np.stack([dbn[e].reshape(8, 128).T for e in range(8)],
                                  axis=1).reshape(128, 8 * 8))
    ridx = np.asarray(i["L0_ffn_router_indices"])
    sh["moe_ridx"] = np.ascontiguousarray(
        ridx.reshape(16, 128, 2).transpose(1, 0, 2).reshape(128, 32),
        dtype=np.int32)
    rw = np.asarray(i["L0_ffn_routing_weights"])
    sh["moe_rw"] = cc(rw.reshape(16, 128, 8).transpose(1, 0, 2).reshape(128, 128))

    sh["m_lnw_pp"] = pp(i["L1_attn_ln_w"], 8)
    sh["m_lnb_pp"] = pp(i["L1_attn_ln_b"], 8)
    sh["m_inproj_t"] = cc(np.asarray(i["L1_attn_in_proj_w"]).T)
    cw = np.asarray(i["L1_attn_conv_w"]).reshape(2048, 4)
    sh["m_convw_pp"] = cc(cw.reshape(16, 128, 4).transpose(1, 0, 2).reshape(128, 64))
    sh["m_convb_pp"] = pp(i["L1_attn_conv_b"], 16)
    sh["m_xproj_t"] = cc(np.asarray(i["L1_attn_x_proj_w"]).T)
    sh["m_dtproj_t"] = cc(np.asarray(i["L1_attn_dt_proj_w"]).T)
    sh["m_dtb_pp"] = pp(i["L1_attn_dt_proj_b"], 16)
    al = np.asarray(i["L1_attn_A_log"])
    sh["m_alog_pp"] = cc(al.reshape(16, 128, 16).transpose(1, 0, 2).reshape(128, 256))
    sh["m_d_pp"] = pp(i["L1_attn_D"], 16)
    sh["m_outproj_t"] = cc(np.asarray(i["L1_attn_out_proj_w"]).T)

    sh["f_gw"] = cc(i["L1_ffn_gate_w"])
    sh["f_gb_t"] = cc(np.broadcast_to(np.asarray(i["L1_ffn_gate_b"]), (128, 8)))
    sh["f_w1"] = cc(i["L1_ffn_w1"])
    sh["f_w2"] = cc(i["L1_ffn_w2"])
    sh["f_wg"] = cc(i["L1_ffn_wg"])
    sh["f_w3"] = cc(i["L1_ffn_w3"])

    def ppk(b, nchunk):
        b = np.asarray(b)
        return cc(np.stack([b[e].reshape(nchunk, 128).T for e in range(8)],
                           axis=1).reshape(128, 8 * nchunk))

    sh["f_b1_pp"] = ppk(i["L1_ffn_b1"], 16)
    sh["f_b2_pp"] = ppk(i["L1_ffn_b2"], 16)
    sh["f_bg_pp"] = ppk(i["L1_ffn_bg"], 16)
    sh["f_b3_pp"] = ppk(i["L1_ffn_b3"], 8)
    return sh


def kernel(**inputs):
    key = (STOP_AFTER, tuple(DEBUG_DUMPS))
    if key not in _NC_CACHE:
        _NC_CACHE[key] = build()
    nc = _NC_CACHE[key]

    sh = _shared_inputs(inputs)
    x_all = np.ascontiguousarray(np.asarray(inputs["L0_attn_x"]), dtype=np.float32)
    in_maps = []
    for c in range(8):
        d = dict(sh)
        d["x"] = np.ascontiguousarray(x_all[c])
        d["moe_mask"] = np.full((128, 1), 1.0 if c == 0 else 0.0, dtype=np.float32)
        in_maps.append(d)

    res = run_bass_kernel_spmd(nc, in_maps, core_ids=list(range(8)))
    out = np.stack([res.results[c]["out"] for c in range(8)], axis=0)
    if DEBUG_DUMPS:
        kernel.last_debug = [
            {nm: res.results[c]["dbg_" + nm] for nm in DEBUG_DUMPS}
            for c in range(8)]
    return out.astype(np.float32)


# version 03 (final)
# speedup vs baseline: 1.0320x; 1.0320x over previous
import numpy as np
from contextlib import ExitStack

import concourse.bass as bass
import concourse.mybir as mybir
from concourse.bass_utils import run_bass_kernel_spmd

F32 = mybir.dt.float32
BF16 = mybir.dt.bfloat16
I32 = mybir.dt.int32
AF = mybir.ActivationFunctionType
ALU = mybir.AluOpType
AX = mybir.AxisListType

T = 2048
D = 1024
NPAIR = 8
DI = 2048
NST = 16
NE = 8
EPS = 1e-5
TC = 256
NCH = T // TC

DEBUG_DUMPS = []
STOP_AFTER = None


class K:

    ENGS = ("tensor", "vector", "scalar", "gpsimd", "sync")

    def __init__(self, nc):
        self.nc = nc
        self.q = {e: [] for e in self.ENGS}
        self.sems = {}
        self.c = {}
        self.maxwait = {}

    def sem(self, st, name):
        self.sems[name] = st.enter_context(self.nc.semaphore(name))
        self.c[name] = 0
        self.maxwait[name] = 0

    def _op(self, eng, fn, inc, k):
        n = None
        if inc is not None:
            self.c[inc] += k
            n = self.c[inc]
            h = self.sems[inc]
            self.q[eng].append(lambda e, fn=fn, h=h, k=k: fn(e).then_inc(h, k))
        else:
            self.q[eng].append(lambda e, fn=fn: fn(e))
        return n

    def t(self, fn, inc=None, k=1):
        return self._op("tensor", fn, inc, k)

    def v(self, fn, inc=None, k=1):
        return self._op("vector", fn, inc, k)

    def s(self, fn, inc=None, k=1):
        return self._op("scalar", fn, inc, k)

    def g(self, fn, inc=None, k=1):
        return self._op("gpsimd", fn, inc, k)

    def y(self, fn, inc=None, k=1):
        return self._op("sync", fn, inc, k)

    def w(self, eng, sem, val):
        if val is None or val <= 0:
            return
        h = self.sems[sem]
        self.maxwait[sem] = max(self.maxwait[sem], val)
        self.q[eng].append(lambda e, h=h, v=val: e.wait_ge(h, v))

    def barrier(self):
        vals = {sn: self.c[sn] for sn in self.sems}
        for e in self.ENGS:
            for sn, v in vals.items():
                self.w(e, sn, v)

    def flush(self):
        for name in self.sems:
            assert self.maxwait[name] <= self.c[name], (
                f"sem {name}: wait {self.maxwait[name]} > inc total {self.c[name]}")
        with self.nc.Block() as b:
            for eng in self.ENGS:
                ops = self.q[eng]
                if ops:
                    getattr(b, eng)(lambda e, ops=ops: [f(e) for f in ops])
        self.q = {e: [] for e in self.ENGS}


def build():
    nc = bass.Bass("TRN2", target_bir_lowering=False)
    din = {}

    def inp(name, shape, dtype=F32):
        din[name] = nc.dram_tensor(name, list(shape), dtype,
                                   kind="ExternalInput").ap()
        return din[name]

    x_in = inp("x", (T, D))
    ident_in = inp("ident", (128, 128))
    oh32b_in = inp("oh32f", (96, 32 * 128))
    oh8f_in = inp("oh8f", (8, 8 * 128))

    wq_in = inp("l0_wq", (D, D))
    wk_in = inp("l0_wk", (D, D))
    wv_in = inp("l0_wv", (D, D))
    wo_in = inp("l0_wo", (D, D))
    bo_in = inp("l0_bo_pp", (128, 8))

    gup_in = inp("moe_gup", (NE, D, 2048))
    gub_in = inp("moe_gub_pp", (128, NE * 16))
    down_in = inp("moe_down", (NE, 1024, D))
    dbp_in = inp("moe_db_pp", (128, NE * 8))
    ridx_in = inp("moe_ridx", (128, 32), I32)
    rw_in = inp("moe_rw", (128, 16 * 8))
    mask_in = inp("moe_mask", (128, 1))

    mlnw_in = inp("m_lnw_pp", (128, 8))
    mlnb_in = inp("m_lnb_pp", (128, 8))
    minp_in = inp("m_inproj_t", (D, 2 * DI))
    mcw_in = inp("m_convw_pp", (128, 64))
    mcb_in = inp("m_convb_pp", (128, 16))
    mxp_in = inp("m_xproj_t", (DI, 96))
    mdt_in = inp("m_dtproj_t", (64, DI))
    mdtb_in = inp("m_dtb_pp", (128, 16))
    malog_in = inp("m_alog_pp", (128, 256))
    mD_in = inp("m_d_pp", (128, 16))
    mout_in = inp("m_outproj_t", (DI, D))

    fgw_in = inp("f_gw", (D, NE))
    fgb_in = inp("f_gb_t", (128, NE))
    fw1_in = inp("f_w1", (NE, D, 2048))
    fw2_in = inp("f_w2", (NE, D, 2048))
    fwg_in = inp("f_wg", (NE, 2048, 2048))
    fw3_in = inp("f_w3", (NE, 2048, D))
    fb1_in = inp("f_b1_pp", (128, NE * 16))
    fb2_in = inp("f_b2_pp", (128, NE * 16))
    fbg_in = inp("f_bg_pp", (128, NE * 16))
    fb3_in = inp("f_b3_pp", (128, NE * 8))

    scr_x1 = nc.dram_tensor("scr_x1", [128, 16384], F32).ap()
    scr_rows = nc.dram_tensor("scr_rows", [1, 4096], F32).ap()
    out_d = nc.dram_tensor("out", [T, D], F32, kind="ExternalOutput").ap()
    dbg = {}
    for nm in DEBUG_DUMPS:
        dbg[nm] = nc.dram_tensor("dbg_" + nm, [128, 16384], F32,
                                 kind="ExternalOutput").ap()

    with ExitStack() as st:
        sbt = lambda n, sh, dt=F32: st.enter_context(
            nc.sbuf_tensor(n, list(sh), dt))
        pst = lambda n: st.enter_context(nc.psum_tensor(n, [128, 512], F32))

        xslot = sbt("xslot", (128, 16384), BF16)
        oh32b = sbt("s_oh32", (96, 32 * 128), BF16)
        oh8f = sbt("s_oh8", (8, 8 * 128))
        identf = sbt("identf", (128, 128))
        identb = sbt("identb", (128, 128), BF16)
        onesf = sbt("onesf", (128, 2))
        onesb = sbt("onesb", (128, 2), BF16)
        onesrow = sbt("onesrow", (128, 128))
        epsc = sbt("epsc", (128, 1))
        bo_pp = sbt("bo_pp", (128, 8))
        gub_pp = sbt("gub_pp", (128, NE * 16))
        db_pp = sbt("db_pp", (128, NE * 8))
        ridx_i = sbt("ridx_i", (128, 32), I32)
        ridx_f = sbt("ridx_f", (128, 32))
        rw_t = sbt("rw_t", (128, 16 * 8))
        mask_pp = sbt("mask_pp", (128, 1))
        m_lnw = sbt("m_lnw", (128, 8))
        m_lnb = sbt("m_lnb", (128, 8))
        m_cw = sbt("m_cw", (128, 64))
        m_cb = sbt("m_cb", (128, 16))
        m_dtb = sbt("m_dtb", (128, 16))
        m_app = sbt("m_app", (128, 256))
        m_dpp = sbt("m_dpp", (128, 16))
        f_gb = sbt("f_gb", (128, NE))
        f_b1 = sbt("f_b1", (128, NE * 16))
        f_b2 = sbt("f_b2", (128, NE * 16))
        f_bg = sbt("f_bg", (128, NE * 16))
        f_b3 = sbt("f_b3", (128, NE * 8))

        pA, pB, pC, pD, pE, pR = (pst("pA"), pst("pB"), pst("pC"),
                                  pst("pD"), pst("pE"), pst("pR"))
        pO = st.enter_context(nc.psum_tensor("pO", [65, 512], F32))

        kk = K(nc)
        for s_ in ("dW", "dX", "T", "V", "S", "G", "B"):
            kk.sem(st, s_)

        def stop_here(tag):
            return STOP_AFTER == tag

        kk.g(lambda e: e.dma_start(oh32b[:, :], oh32b_in), inc="dW", k=16)
        kk.w("tensor", "dW", kk.c["dW"])
        for dst, src in [
            (identf[:, :], ident_in),
            (oh8f[:, :], oh8f_in), (bo_pp[:, :], bo_in),
            (gub_pp[:, :], gub_in), (db_pp[:, :], dbp_in),
            (ridx_i[:, :], ridx_in), (rw_t[:, :], rw_in),
            (mask_pp[:, :], mask_in), (m_lnw[:, :], mlnw_in),
            (m_lnb[:, :], mlnb_in), (m_cw[:, :], mcw_in), (m_cb[:, :], mcb_in),
            (m_dtb[:, :], mdtb_in), (m_app[:, :], malog_in),
            (m_dpp[:, :], mD_in), (f_gb[:, :], fgb_in), (f_b1[:, :], fb1_in),
            (f_b2[:, :], fb2_in), (f_bg[:, :], fbg_in), (f_b3[:, :], fb3_in),
        ]:
            kk.y(lambda e, d=dst, s_=src: e.dma_start(d, s_), inc="dX", k=16)
        cdma = kk.c["dX"]
        kk.w("vector", "dX", cdma)
        kk.w("scalar", "dX", cdma)
        kk.v(lambda e: e.memset(onesf[:, :], 1.0))
        kk.v(lambda e: e.memset(onesrow[:, :], 1.0))
        kk.v(lambda e: e.memset(epsc[:, :], EPS))
        kk.v(lambda e: e.memset(onesb[:, :], 1.0))
        kk.v(lambda e: e.tensor_copy(identb[:, :], identf[:, :]))
        kk.v(lambda e: e.tensor_copy(ridx_f[:, :], ridx_i[:, :]), inc="V")
        kk.s(lambda e: e.activation(m_app[:, :], m_app[:, :], AF.Exp), inc="S")
        kk.w("vector", "S", kk.c["S"])
        kk.v(lambda e: e.tensor_scalar(m_app[:, :], m_app[:, :], -1.0, None,
                                       ALU.mult), inc="V")

        accA = sbt("accA", (128, 16384))

        with ExitStack() as ph:
            xt = [ph.enter_context(nc.sbuf_tensor(f"p1x{i}", [128, D], F32))
                  for i in range(2)]
            sqt = ph.enter_context(nc.sbuf_tensor("p1sq", [128, D], F32))
            stt = ph.enter_context(nc.sbuf_tensor("p1st", [128, 96], F32))
            xnb = ph.enter_context(nc.sbuf_tensor("p1xn", [128, D], BF16))

            used = [0, 0]
            for tt in range(16):
                b = tt % 2
                kk.w("sync", "S", used[b])
                kk.y(lambda e, b=b, tt=tt: e.dma_start(
                    xt[b][:, :], x_in[tt * 128:(tt + 1) * 128, :]), inc="dX", k=16)
                ld = kk.c["dX"]
                kk.w("vector", "dX", ld)
                kk.v(lambda e, b=b, tt=tt: e.reduce_sum(
                    stt[:, tt:tt + 1], xt[b][:, :], AX.X), inc="V")
                vred = kk.c["V"]
                kk.w("scalar", "dX", ld)
                kk.w("scalar", "V", vred)
                kk.s(lambda e, b=b, tt=tt: e.activation(
                    sqt[:, :], xt[b][:, :], AF.Square,
                    accum_out=stt[:, 16 + tt:17 + tt]), inc="S")
                used[b] = kk.c["S"]
            kk.w("vector", "S", kk.c["S"])
            kk.v(lambda e: e.tensor_scalar(stt[:, 32:48], stt[:, 0:16], 1.0 / D,
                                           None, ALU.mult))
            kk.v(lambda e: e.tensor_scalar(stt[:, 48:64], stt[:, 16:32], 1.0 / D,
                                           None, ALU.mult), inc="V")
            kk.w("scalar", "V", kk.c["V"])
            kk.s(lambda e: e.activation(stt[:, 64:80], stt[:, 32:48], AF.Square),
                 inc="S")
            kk.w("vector", "S", kk.c["S"])
            kk.v(lambda e: e.tensor_sub(stt[:, 48:64], stt[:, 48:64],
                                        stt[:, 64:80]), inc="V")
            kk.w("scalar", "V", kk.c["V"])
            kk.s(lambda e: e.activation(stt[:, 64:80], stt[:, 48:64], AF.Sqrt,
                                        bias=epsc[:, 0:1]), inc="S")
            kk.w("vector", "S", kk.c["S"])
            kk.v(lambda e: e.reciprocal(stt[:, 80:96], stt[:, 64:80]), inc="V")
            stats_ok = kk.c["V"]
            passA_S = kk.c["S"]

            used = [0, 0]
            tev = [0, 0]
            last_tp = 0
            for tt in range(16):
                b = tt % 2
                kk.w("sync", "V", used[b])
                if tt < 2:
                    kk.w("sync", "S", passA_S)
                    kk.w("sync", "V", stats_ok)
                kk.y(lambda e, b=b, tt=tt: e.dma_start(
                    xt[b][:, :], x_in[tt * 128:(tt + 1) * 128, :]), inc="dX", k=16)
                ld = kk.c["dX"]
                kk.w("vector", "dX", ld)
                kk.w("vector", "V", stats_ok)
                kk.w("vector", "T", last_tp)
                kk.v(lambda e, b=b, tt=tt: e.tensor_scalar(
                    xnb[:, :], xt[b][:, :], stt[:, 32 + tt:33 + tt],
                    stt[:, 80 + tt:81 + tt], ALU.subtract, ALU.mult), inc="V")
                xn_ok = kk.c["V"]
                used[b] = xn_ok
                for r in range(8):
                    pt = (pA, pB)[r % 2]
                    ptb = pt.ap().bitcast(BF16)[:, 0:128]
                    kk.w("tensor", "V", xn_ok)
                    kk.w("tensor", "V", tev[r % 2])
                    kk.t(lambda e, ptb=ptb, r=r: e.transpose(
                        ptb, xnb[:, r * 128:(r + 1) * 128], identb[:, :]),
                        inc="T")
                    td = kk.c["T"]
                    kk.w("vector", "T", td)
                    kk.v(lambda e, ptb=ptb, r=r, tt=tt: e.tensor_copy(
                        xslot[:, r * 2048 + tt * 128: r * 2048 + (tt + 1) * 128],
                        ptb), inc="V")
                    tev[r % 2] = kk.c["V"]
                last_tp = kk.c["T"]
            kk.barrier()
            if "p1" in dbg:
                kk.g(lambda e: e.dma_start(dbg["p1"], xslot[:, :]), inc="dW", k=16)
                kk.w("gpsimd", "dW", kk.c["dW"])
                kk.barrier()
            kk.flush()

        if not stop_here("p1"):
          with ExitStack() as ph:
            wqkv = [ph.enter_context(nc.sbuf_tensor(f"p2w{i}", [128, 4096], BF16))
                    for i in range(2)]
            qp = [ph.enter_context(nc.sbuf_tensor(f"p2q{i}", [128, T], BF16))
                  for i in range(2)]
            kp = [ph.enter_context(nc.sbuf_tensor(f"p2k{i}", [128, T], BF16))
                  for i in range(2)]
            vp = [ph.enter_context(nc.sbuf_tensor(f"p2v{i}", [128, 2080], BF16))
                  for i in range(2)]
            sqb = ph.enter_context(nc.sbuf_tensor("p2sqb", [128, T], BF16))
            ptl = [ph.enter_context(nc.sbuf_tensor(f"p2p{i}", [128, 512], BF16))
                   for i in range(2)]
            aTp = ph.enter_context(nc.sbuf_tensor("p2aT", [128, T], BF16))
            aTt = ph.enter_context(nc.sbuf_tensor("p2aTt", [64, T], BF16))
            rbx = ph.enter_context(nc.sbuf_tensor("p2rb", [64, 512], F32))
            rip = ph.enter_context(nc.sbuf_tensor("p2rip", [128, 512], F32))
            nrm = ph.enter_context(nc.sbuf_tensor("p2nrm", [1, 16], F32))
            cbias = ph.enter_context(nc.sbuf_tensor("p2cb", [128, 2], F32))

            def qkvo_load(p, buf):
                for j, w_ in enumerate((wq_in, wk_in, wv_in)):
                    dst = wqkv[buf][:, j * 1024:(j + 1) * 1024].rearrange(
                        "q (kc m) -> q kc m", kc=8)
                    kk.g(lambda e, dst=dst, w_=w_, p=p: e.dma_start(
                        dst, w_.rearrange("(kc q) m -> q kc m", q=128)
                        [:, :, p * 128:(p + 1) * 128]), inc="dW", k=16)
                kk.g(lambda e, buf=buf, p=p: e.dma_start(
                    wqkv[buf][:, 3 * 1024:4 * 1024],
                    wo_in[p * 128:(p + 1) * 128, :]), inc="dW", k=16)
                return kk.c["dW"]

            wld = [qkvo_load(0, 0), 0]
            pairT = [0, 0]
            qk_ev = [0, 0]
            v_ev = [0, 0]
            exp_ev = [0, 0]
            pv_mm = [0, 0]
            op_ev = [0, 0]
            aT_used_V = 0
            aT_dma = 0

            for p in range(NPAIR):
                buf = p % 2
                if p + 1 < NPAIR:
                    kk.w("gpsimd", "T", pairT[(p + 1) % 2])
                    wld[(p + 1) % 2] = qkvo_load(p + 1, (p + 1) % 2)

                kk.w("tensor", "dW", wld[buf])

                for which, wofs, dst in (("q", 0, qp), ("k", 1, kp)):
                    for tg in range(4):
                        pt = (pA, pB)[tg % 2]
                        kk.w("tensor", "S", qk_ev[tg % 2])
                        for kc in range(8):
                            kk.t(lambda e, pt=pt, wofs=wofs, kc=kc, tg=tg, buf=buf:
                                 e.matmul(pt[:, :],
                                          wqkv[buf][:, wofs * 1024 + kc * 128:
                                                    wofs * 1024 + (kc + 1) * 128],
                                          xslot[:, kc * 2048 + tg * 512:
                                                kc * 2048 + (tg + 1) * 512],
                                          start=(kc == 0), stop=(kc == 7)),
                                 inc="T" if kc == 7 else None)
                        mm = kk.c["T"]
                        kk.w("scalar", "T", mm)
                        kk.s(lambda e, pt=pt, tg=tg, buf=buf, dst=dst: e.copy(
                            dst[buf][:, tg * 512:(tg + 1) * 512], pt[:, :]),
                            inc="S")
                        qk_ev[tg % 2] = kk.c["S"]
                qk_done = kk.c["S"]

                for tt in range(16):
                    pt = (pC, pD)[tt % 2]
                    kk.w("tensor", "V", v_ev[tt % 2])
                    for kc in range(8):
                        kk.t(lambda e, pt=pt, kc=kc, tt=tt, buf=buf: e.matmul(
                            pt[:, 0:128],
                            xslot[:, kc * 2048 + tt * 128: kc * 2048 + (tt + 1) * 128],
                            wqkv[buf][:, 2 * 1024 + kc * 128:
                                      2 * 1024 + (kc + 1) * 128],
                            start=(kc == 0), stop=(kc == 7)),
                            inc="T" if kc == 7 else None)
                    mm = kk.c["T"]
                    kk.w("vector", "T", mm)
                    for hl in range(2):
                        kk.v(lambda e, pt=pt, tt=tt, hl=hl, buf=buf: e.tensor_copy(
                            vp[buf][:, tt * 130 + hl * 65: tt * 130 + hl * 65 + 64],
                            pt[:, hl * 64: hl * 64 + 64]),
                            inc="V" if hl == 1 else None)
                    v_ev[tt % 2] = kk.c["V"]
                for hl in range(2):
                    kk.v(lambda e, hl=hl, buf=buf: e.memset(
                        vp[buf].ap().rearrange("q (tt c) -> q tt c", tt=16)
                        [:, :, hl * 65 + 64: hl * 65 + 65], 1.0), inc="V")
                v_done = kk.c["V"]

                for j, srcb in ((0, qp), (8, kp)):
                    kk.w("scalar", "S", qk_done)
                    kk.w("scalar", "T", kk.c["T"])
                    kk.s(lambda e, srcb=srcb, buf=buf: e.activation(
                        sqb[:, :], srcb[buf][:, :], AF.Square), inc="S")
                    sq_ok = kk.c["S"]
                    for hl in range(2):
                        for tg in range(4):
                            kk.w("tensor", "S", sq_ok)
                            if j + hl + tg > 0:
                                kk.w("tensor", "V", kk.c["V"])
                            kk.t(lambda e, hl=hl, tg=tg: e.matmul(
                                pR[0:1, :], onesb[hl * 64: hl * 64 + 64, 0:1],
                                sqb[hl * 64: hl * 64 + 64, tg * 512:(tg + 1) * 512],
                                start=True, stop=True), inc="T")
                            mt = kk.c["T"]
                            kk.w("vector", "T", mt)
                            kk.v(lambda e, j=j, hl=hl, tg=tg: e.reduce_max(
                                nrm[0:1, j + hl * 4 + tg: j + hl * 4 + tg + 1],
                                pR[0:1, :], AX.X), inc="V")
                kk.w("vector", "V", kk.c["V"])
                kk.v(lambda e: e.reduce_max(nrm[0:1, 0:1], nrm[0:1, 0:4], AX.X))
                kk.v(lambda e: e.reduce_max(nrm[0:1, 1:2], nrm[0:1, 4:8], AX.X))
                kk.v(lambda e: e.reduce_max(nrm[0:1, 2:3], nrm[0:1, 8:12], AX.X))
                kk.v(lambda e: e.reduce_max(nrm[0:1, 3:4], nrm[0:1, 12:16],
                                            AX.X), inc="V")
                kk.w("vector", "V", kk.c["V"])
                kk.v(lambda e: e.tensor_add(nrm[0:1, 4:5], nrm[0:1, 0:1],
                                            nrm[0:1, 2:3]))
                kk.v(lambda e: e.tensor_add(nrm[0:1, 5:6], nrm[0:1, 1:2],
                                            nrm[0:1, 3:4]), inc="V")
                kk.w("vector", "V", kk.c["V"])
                kk.v(lambda e: e.tensor_scalar(nrm[0:1, 4:6], nrm[0:1, 4:6],
                                               -0.0625, None, ALU.mult), inc="V")
                nrm_ok = kk.c["V"]
                kk.w("tensor", "V", nrm_ok)
                for hl in range(2):
                    kk.t(lambda e, hl=hl: e.matmul(
                        pR[:, 0:1], oh8f[0:1, 0:128], nrm[0:1, 4 + hl:5 + hl],
                        start=True, stop=True), inc="T")
                    mt = kk.c["T"]
                    kk.w("scalar", "T", mt)
                    kk.s(lambda e, hl=hl: e.copy(cbias[:, hl:hl + 1], pR[:, 0:1]),
                         inc="S")
                cb_ok = kk.c["S"]
                kk.w("scalar", "S", cb_ok)

                kk.w("tensor", "S", cb_ok)
                kk.w("tensor", "V", v_done)
                kk.w("vector", "dX", aT_dma)
                for hl in range(2):
                    hb = hl * 64
                    for tg in range(4):
                        for kc in range(16):
                            pss = (pA, pB)[kc % 2]
                            kk.w("tensor", "S", exp_ev[kc % 2])
                            kk.t(lambda e, pss=pss, hb=hb, kc=kc, tg=tg, buf=buf:
                                 e.matmul(
                                     pss[:, :],
                                     kp[buf][hb:hb + 64, kc * 128:(kc + 1) * 128],
                                     qp[buf][hb:hb + 64, tg * 512:(tg + 1) * 512],
                                     start=True, stop=True), inc="T")
                            smm = kk.c["T"]
                            kk.w("scalar", "T", smm)
                            kk.w("scalar", "T", pv_mm[kc % 2])
                            kk.s(lambda e, pss=pss, kc=kc, hl=hl: e.activation(
                                ptl[kc % 2][:, :], pss[:, :], AF.Exp,
                                bias=cbias[:, hl:hl + 1], scale=0.125), inc="S")
                            exp_ev[kc % 2] = kk.c["S"]
                            kk.w("tensor", "S", exp_ev[kc % 2])
                            if kc == 0:
                                kk.w("tensor", "V", aT_used_V)
                            kk.t(lambda e, kc=kc, hl=hl, buf=buf: e.matmul(
                                pO[:, :],
                                vp[buf][:, kc * 130 + hl * 65:
                                        kc * 130 + hl * 65 + 65],
                                ptl[kc % 2][:, :],
                                start=(kc == 0), stop=(kc == 15),
                                skip_group_check=True), inc="T")
                            pv_mm[kc % 2] = kk.c["T"]
                        pv_done = kk.c["T"]
                        kk.w("vector", "T", pv_done)
                        kk.v(lambda e: e.reciprocal(rip[64:65, :], pO[64:65, :]),
                             inc="V")
                        rip_ok = kk.c["V"]
                        kk.w("tensor", "V", rip_ok)
                        kk.t(lambda e: e.matmul(pR[0:64, :], onesrow[64:65, 0:64],
                                                rip[64:65, :], start=True,
                                                stop=True), inc="T")
                        rmm = kk.c["T"]
                        kk.w("scalar", "T", rmm)
                        kk.s(lambda e: e.copy(rbx[:, :], pR[0:64, :]), inc="S")
                        rev = kk.c["S"]
                        kk.w("vector", "S", rev)
                        if hl == 0:
                            dst = aTp[0:64, tg * 512:(tg + 1) * 512]
                        else:
                            dst = aTt[0:64, tg * 512:(tg + 1) * 512]
                        kk.v(lambda e, dst=dst: e.tensor_mul(
                            dst, pO[0:64, :], rbx[:, :]), inc="V")
                        aT_used_V = kk.c["V"]
                aT_done = kk.c["V"]
                kk.w("sync", "V", aT_done)
                kk.y(lambda e: e.dma_start(aTp[64:128, :], aTt[0:64, :]),
                     inc="dX", k=16)
                aT_dma = kk.c["dX"]

                kk.w("tensor", "V", aT_done)
                kk.w("tensor", "dX", aT_dma)
                for dc in range(8):
                    for tg in range(4):
                        i = dc * 4 + tg
                        pt = (pC, pD)[i % 2]
                        kk.w("tensor", "V", op_ev[i % 2])
                        kk.t(lambda e, pt=pt, dc=dc, tg=tg, buf=buf: e.matmul(
                            pt[:, :],
                            wqkv[buf][:, 3 * 1024 + dc * 128:
                                      3 * 1024 + (dc + 1) * 128],
                            aTp[:, tg * 512:(tg + 1) * 512],
                            start=True, stop=True), inc="T")
                        omm = kk.c["T"]
                        dst = accA[:, dc * 2048 + tg * 512:
                                   dc * 2048 + (tg + 1) * 512]
                        if p == 0:
                            kk.w("vector", "T", omm)
                            kk.v(lambda e, pt=pt, dc=dc, dst=dst: e.tensor_scalar(
                                dst, pt[:, :], bo_pp[:, dc:dc + 1], None,
                                ALU.add), inc="V")
                        else:
                            kk.w("vector", "T", omm)
                            kk.v(lambda e, pt=pt, dst=dst: e.tensor_add(
                                dst, dst, pt[:, :]), inc="V")
                        op_ev[i % 2] = kk.c["V"]
                pairT[buf] = kk.c["T"]

            kk.barrier()
            if "p2" in dbg:
                kk.y(lambda e: e.dma_start(dbg["p2"], accA[:, :]), inc="dX", k=16)
                kk.w("sync", "dX", kk.c["dX"])
                kk.barrier()
            kk.flush()

        if not stop_here("p1") and not stop_here("p2"):
          with ExitStack() as ph:
            gup_sb = ph.enter_context(nc.sbuf_tensor("p3gup", [128, 16384], BF16))
            down_sb = ph.enter_context(nc.sbuf_tensor("p3dwn", [128, 8192], BF16))
            glu_sb = ph.enter_context(nc.sbuf_tensor("p3glu", [128, 16384], BF16))
            cmat = ph.enter_context(nc.sbuf_tensor("p3cm", [128, 128], F32))
            cTt = ph.enter_context(nc.sbuf_tensor("p3cT", [8, 2048], F32))
            gbuf = ph.enter_context(nc.sbuf_tensor("p3g", [128, 512], F32))
            sgm = ph.enter_context(nc.sbuf_tensor("p3s", [128, 512], F32))
            scr = ph.enter_context(nc.sbuf_tensor("p3sc", [128, 64], F32))

            rfv = ridx_f.ap().rearrange("q (tt j) -> q tt j", j=2)
            rwv = rw_t.ap().rearrange("q (tt e) -> q tt e", e=8)
            cmv = cmat.ap().rearrange("q (tt e) -> q tt e", e=8)
            sc2 = scr.ap()[:, 0:32].rearrange("q (tt j) -> q tt j", j=2)
            kk.v(lambda en: en.tensor_copy(scr[:, 0:32], ridx_f[:, :]),
                 inc="V")
            kk.w("vector", "V", kk.c["V"])
            for e_ in range(NE):
                kk.v(lambda en, e_=e_: en.tensor_scalar(
                    scr[:, 0:32], ridx_f[:, :], e_ - 0.5, None, ALU.is_ge))
                kk.v(lambda en, e_=e_: en.tensor_scalar(
                    scr[:, 32:64], ridx_f[:, :], e_ + 0.5, None, ALU.is_ge),
                    inc="V")
                kk.w("vector", "V", kk.c["V"])
                kk.v(lambda en: en.tensor_sub(scr[:, 0:32], scr[:, 0:32],
                                              scr[:, 32:64]), inc="V")
                kk.w("vector", "V", kk.c["V"])
                kk.v(lambda en: en.tensor_add(sc2[:, :, 0:1], sc2[:, :, 0:1],
                                              sc2[:, :, 1:2]), inc="V")
                kk.w("vector", "V", kk.c["V"])
                kk.v(lambda en, e_=e_: en.scalar_tensor_tensor(
                    cmv[:, :, e_:e_ + 1], sc2[:, :, 0:1], mask_pp[:, 0:1],
                    rwv[:, :, e_:e_ + 1], ALU.mult, ALU.mult), inc="V")
            cm_ok = kk.c["V"]
            ctev = [0, 0]
            for tt in range(16):
                pt = (pE, pR)[tt % 2]
                kk.w("tensor", "V", cm_ok)
                kk.w("tensor", "V", ctev[tt % 2])
                kk.t(lambda e, pt=pt, tt=tt: e.transpose(
                    pt[0:8, 0:128], cmat[:, tt * 8:(tt + 1) * 8], identf[:, :]),
                    inc="T")
                mt = kk.c["T"]
                kk.w("vector", "T", mt)
                kk.v(lambda e, pt=pt, tt=tt: e.tensor_copy(
                    cTt[0:8, tt * 128:(tt + 1) * 128], pt[0:8, 0:128]), inc="V")
                ctev[tt % 2] = kk.c["V"]
            cT_ok = kk.c["V"]

            def moe_gup_load(e_, gate):
                kk.w("gpsimd", "T", gate)
                kk.g(lambda en, e_=e_: en.dma_start(
                    gup_sb.ap().rearrange("q (kc m) -> q kc m", kc=8),
                    gup_in[e_].rearrange("(kc q) m -> q kc m", q=128)),
                    inc="dW", k=16)
                return kk.c["dW"]

            def moe_down_load(e_, gate):
                kk.w("gpsimd", "T", gate)
                kk.g(lambda en, e_=e_: en.dma_start(
                    down_sb.ap().rearrange("q (kc m) -> q kc m", kc=8),
                    down_in[e_].rearrange("(kc q) m -> q kc m", q=128)),
                    inc="dW", k=16)
                return kk.c["dW"]

            gup_ld = moe_gup_load(0, 0)
            down_ld = moe_down_load(0, 0)
            glu_ev = [0, 0]
            dn_ev = [0, 0]
            for e_ in range(NE):
                kk.w("tensor", "dW", gup_ld)
                kk.w("tensor", "V", dn_ev[0])
                kk.w("tensor", "V", dn_ev[1])
                for fc in range(8):
                    for tg in range(4):
                        pg = (pA, pB)[tg % 2]
                        pu = (pC, pD)[tg % 2]
                        kk.w("tensor", "V", glu_ev[tg % 2])
                        for kc in range(8):
                            kk.t(lambda en, pg=pg, kc=kc, fc=fc, tg=tg: en.matmul(
                                pg[:, :],
                                gup_sb[:, kc * 2048 + fc * 128:
                                       kc * 2048 + (fc + 1) * 128],
                                xslot[:, kc * 2048 + tg * 512:
                                      kc * 2048 + (tg + 1) * 512],
                                start=(kc == 0), stop=(kc == 7)))
                        for kc in range(8):
                            kk.t(lambda en, pu=pu, kc=kc, fc=fc, tg=tg: en.matmul(
                                pu[:, :],
                                gup_sb[:, kc * 2048 + 1024 + fc * 128:
                                       kc * 2048 + 1024 + (fc + 1) * 128],
                                xslot[:, kc * 2048 + tg * 512:
                                      kc * 2048 + (tg + 1) * 512],
                                start=(kc == 0), stop=(kc == 7)),
                                inc="T" if kc == 7 else None)
                        mm = kk.c["T"]
                        kk.w("vector", "T", mm)
                        kk.v(lambda en, pg=pg, e_=e_, fc=fc: en.tensor_scalar(
                            gbuf[:, :], pg[:, :],
                            gub_pp[:, e_ * 16 + fc: e_ * 16 + fc + 1], 7.0,
                            ALU.add, ALU.min), inc="V")
                        g_ok = kk.c["V"]
                        kk.w("scalar", "V", g_ok)
                        kk.s(lambda en: en.activation(sgm[:, :], gbuf[:, :],
                                                      AF.Sigmoid, scale=1.702),
                             inc="S")
                        sg_ok = kk.c["S"]
                        kk.w("vector", "S", sg_ok)
                        kk.v(lambda en: en.tensor_mul(gbuf[:, :], gbuf[:, :],
                                                      sgm[:, :]))
                        kk.v(lambda en, pu=pu, e_=e_, fc=fc: en.tensor_scalar(
                            sgm[:, :], pu[:, :],
                            gub_pp[:, e_ * 16 + 8 + fc: e_ * 16 + 9 + fc], 7.0,
                            ALU.add, ALU.min))
                        kk.v(lambda en: en.tensor_scalar(
                            sgm[:, :], sgm[:, :], -7.0, 1.0, ALU.max, ALU.add))
                        kk.v(lambda en, fc=fc, tg=tg: en.tensor_mul(
                            glu_sb[:, fc * 2048 + tg * 512:
                                   fc * 2048 + (tg + 1) * 512],
                            sgm[:, :], gbuf[:, :]), inc="V")
                        glu_ev[tg % 2] = kk.c["V"]
                glu_done = kk.c["V"]
                gup_T = kk.c["T"]
                if e_ == 0 and "p3g" in dbg:
                    kk.w("gpsimd", "V", glu_done)
                    kk.g(lambda e: e.dma_start(dbg["p3g"], glu_sb[:, :]),
                         inc="dW", k=16)
                    gup_T = gup_T
                if e_ == 0 and "p3c" in dbg:
                    kk.w("sync", "V", cT_ok)
                    kk.y(lambda e: e.dma_start(dbg["p3c"][0:8, 0:2048], cTt[:, :]),
                         inc="dX", k=16)
                if e_ + 1 < NE:
                    gup_ld = moe_gup_load(e_ + 1, gup_T)

                cb_banks = (pC, pD, pE, pR)
                kk.w("tensor", "V", glu_done)
                kk.w("tensor", "V", dn_ev[0])
                kk.w("tensor", "V", dn_ev[1])
                for tg in range(4):
                    kk.t(lambda en, pt=cb_banks[tg], e_=e_, tg=tg: en.matmul(
                        pt[:, :], oh8f[0:8, e_ * 128:(e_ + 1) * 128],
                        cTt[0:8, tg * 512:(tg + 1) * 512], start=True, stop=True),
                        inc="T")
                cb_mm = kk.c["T"]

                kk.w("tensor", "dW", down_ld)
                for hc in range(8):
                    for tg in range(4):
                        i = hc * 4 + tg
                        pt = (pA, pB)[i % 2]
                        kk.w("tensor", "V", dn_ev[i % 2])
                        for ic in range(8):
                            kk.t(lambda en, pt=pt, ic=ic, hc=hc, tg=tg: en.matmul(
                                pt[:, :],
                                down_sb[:, ic * 1024 + hc * 128:
                                        ic * 1024 + (hc + 1) * 128],
                                glu_sb[:, ic * 2048 + tg * 512:
                                       ic * 2048 + (tg + 1) * 512],
                                start=(ic == 0), stop=(ic == 7)),
                                inc="T" if ic == 7 else None)
                        mm = kk.c["T"]
                        kk.w("vector", "T", mm)
                        kk.w("vector", "T", cb_mm)
                        kk.v(lambda en, pt=pt, e_=e_, hc=hc: en.tensor_scalar(
                            gbuf[:, :], pt[:, :],
                            db_pp[:, e_ * 8 + hc: e_ * 8 + hc + 1], None, ALU.add))
                        kk.v(lambda en, cb=cb_banks[tg]: en.tensor_mul(
                            gbuf[:, :], gbuf[:, :], cb[:, :]))
                        dst = accA[:, hc * 2048 + tg * 512:
                                   hc * 2048 + (tg + 1) * 512]
                        kk.v(lambda en, dst=dst: en.tensor_add(dst, dst,
                                                               gbuf[:, :]),
                             inc="V")
                        dn_ev[i % 2] = kk.c["V"]
                down_T = kk.c["T"]
                if e_ + 1 < NE:
                    down_ld = moe_down_load(e_ + 1, down_T)

            kk.barrier()
            if "p3" in dbg:
                kk.y(lambda e: e.dma_start(dbg["p3"], accA[:, :]), inc="dX", k=16)
                kk.w("sync", "dX", kk.c["dX"])
                kk.barrier()
            kk.flush()

        if STOP_AFTER not in ("p1", "p2", "p3"):
          with ExitStack() as ph:
            xt = [ph.enter_context(nc.sbuf_tensor(f"p4x{i}", [128, D], F32))
                  for i in range(2)]
            used = [0, 0]
            tev = [0, 0]
            for tt in range(16):
                b = tt % 2
                kk.w("sync", "V", used[b])
                kk.y(lambda e, b=b, tt=tt: e.dma_start(
                    xt[b][:, :], x_in[tt * 128:(tt + 1) * 128, :]), inc="dX", k=16)
                ld = kk.c["dX"]
                kk.w("tensor", "dX", ld)
                for r in range(8):
                    pt = (pA, pB)[r % 2]
                    kk.w("tensor", "V", tev[r % 2])
                    kk.t(lambda e, pt=pt, b=b, r=r: e.transpose(
                        pt[:, 0:128], xt[b][:, r * 128:(r + 1) * 128],
                        identf[:, :]), inc="T")
                    td = kk.c["T"]
                    kk.w("vector", "T", td)
                    dst = accA[:, r * 2048 + tt * 128: r * 2048 + (tt + 1) * 128]
                    kk.v(lambda e, pt=pt, dst=dst: e.tensor_add(
                        dst, dst, pt[:, 0:128]), inc="V")
                    tev[r % 2] = kk.c["V"]
                used[b] = kk.c["V"]
            kk.barrier()
            if "p4" in dbg:
                kk.y(lambda e: e.dma_start(dbg["p4"], accA[:, :]), inc="dX", k=16)
                kk.w("sync", "dX", kk.c["dX"])
                kk.barrier()
            kk.flush()

        if STOP_AFTER not in ("p1", "p2", "p3", "p4"):
          with ExitStack() as ph:
            sq2 = ph.enter_context(nc.sbuf_tensor("p5sq", [128, 2048], F32))
            rows = ph.enter_context(nc.sbuf_tensor("p5rows", [1, 4096], F32))
            mbb = ph.enter_context(nc.sbuf_tensor("p5mb", [128, 2048], F32))
            rbb = ph.enter_context(nc.sbuf_tensor("p5rb", [128, 2048], F32))
            bnc = ph.enter_context(nc.sbuf_tensor("p5bn", [128, 2048], F32))

            kk.y(lambda e: e.dma_start(scr_x1, accA[:, :]), inc="dX", k=16)
            spill = kk.c["dX"]

            def fm_stats(src):
                evS = [0, 0]
                sqev = [0, 0]
                for tg in range(4):
                    for r in range(8):
                        if r == 0:
                            kk.w("tensor", "S", evS[0])
                        kk.t(lambda e, r=r, tg=tg, src=src: e.matmul(
                            pR[0:1, :], onesf[:, 0:1],
                            src[:, r * 2048 + tg * 512: r * 2048 + (tg + 1) * 512],
                            start=(r == 0), stop=(r == 7)),
                            inc="T" if r == 7 else None)
                    mt = kk.c["T"]
                    kk.w("scalar", "T", mt)
                    kk.s(lambda e, tg=tg: e.copy(
                        rows[0:1, tg * 512:(tg + 1) * 512], pR[0:1, :]), inc="S")
                    evS[0] = kk.c["S"]
                    for r in range(8):
                        sl = sq2[:, (r % 2) * 512:(r % 2) * 512 + 512]
                        kk.w("scalar", "T", sqev[r % 2])
                        if r == 0:
                            kk.w("tensor", "S", evS[1])
                        kk.s(lambda e, r=r, tg=tg, src=src, sl=sl: e.activation(
                            sl, src[:, r * 2048 + tg * 512:
                                    r * 2048 + (tg + 1) * 512], AF.Square),
                            inc="S")
                        sv = kk.c["S"]
                        kk.w("tensor", "S", sv)
                        kk.t(lambda e, r=r, sl=sl: e.matmul(
                            pE[0:1, :], onesf[:, 0:1], sl,
                            start=(r == 0), stop=(r == 7)), inc="T")
                        sqev[r % 2] = kk.c["T"]
                    mt = kk.c["T"]
                    kk.w("scalar", "T", mt)
                    kk.s(lambda e, tg=tg: e.copy(
                        rows[0:1, 2048 + tg * 512: 2048 + (tg + 1) * 512],
                        pE[0:1, :]), inc="S")
                    evS[1] = kk.c["S"]
                sums_ok = kk.c["S"]
                kk.w("vector", "S", sums_ok)
                mr = rows[0:1, 0:2048]
                rr = rows[0:1, 2048:4096]
                kk.v(lambda e, mr=mr: e.tensor_scalar(mr, mr, 1.0 / D, None,
                                                      ALU.mult))
                kk.v(lambda e, rr=rr: e.tensor_scalar(rr, rr, 1.0 / D, None,
                                                      ALU.mult), inc="V")
                kk.w("scalar", "V", kk.c["V"])
                kk.s(lambda e, mr=mr: e.activation(sq2[0:1, 0:2048], mr,
                                                   AF.Square), inc="S")
                kk.w("vector", "S", kk.c["S"])
                kk.v(lambda e, rr=rr: e.tensor_sub(rr, rr, sq2[0:1, 0:2048]),
                     inc="V")
                kk.w("scalar", "V", kk.c["V"])
                kk.s(lambda e, rr=rr: e.activation(sq2[0:1, 0:2048], rr, AF.Sqrt,
                                                   bias=epsc[0:1, 0:1]), inc="S")
                kk.w("vector", "S", kk.c["S"])
                kk.v(lambda e, rr=rr: e.reciprocal(rr, sq2[0:1, 0:2048]), inc="V")
                return kk.c["V"]

            def bcast_rows(gate_v):
                evS = [0, 0]
                for j, dst in ((0, mbb), (1, rbb)):
                    for tg in range(4):
                        pt = (pE, pR)[tg % 2]
                        kk.w("tensor", "V", gate_v)
                        kk.w("tensor", "S", evS[tg % 2])
                        kk.t(lambda e, pt=pt, j=j, tg=tg: e.matmul(
                            pt[:, :], onesrow[0:1, 0:128],
                            rows[0:1, j * 2048 + tg * 512:
                                 j * 2048 + (tg + 1) * 512],
                            start=True, stop=True), inc="T")
                        mt = kk.c["T"]
                        kk.w("scalar", "T", mt)
                        kk.s(lambda e, pt=pt, dst=dst, tg=tg: e.copy(
                            dst[:, tg * 512:(tg + 1) * 512], pt[:, :]), inc="S")
                        evS[tg % 2] = kk.c["S"]
                return kk.c["S"]

            st1 = fm_stats(accA)
            if "p5r" in dbg:
                kk.w("sync", "V", st1)
                kk.y(lambda e: e.dma_start(dbg["p5r"][0:1, 0:4096], rows[0:1, :]),
                     inc="dX", k=16)
            b1 = bcast_rows(st1)
            kk.w("sync", "V", st1)
            kk.y(lambda e: e.dma_start(scr_rows, rows[0:1, :]), inc="dX", k=16)
            rowspill = kk.c["dX"]
            kk.w("vector", "S", b1)
            kk.w("vector", "dX", spill)
            sub_ok = 0
            for r in range(8):
                sl = accA[:, r * 2048:(r + 1) * 2048]
                kk.v(lambda e, sl=sl: e.tensor_sub(bnc[:, :], sl, mbb[:, :]))
                kk.v(lambda e, sl=sl: e.tensor_mul(sl, bnc[:, :], rbb[:, :]),
                     inc="V")
            xn2_ok = kk.c["V"]
            accB = accA

            kk.w("tensor", "V", xn2_ok)
            kk.w("scalar", "dX", rowspill)
            st2 = fm_stats(accB)
            b2 = bcast_rows(st2)
            kk.w("vector", "S", b2)
            for r in range(8):
                sl = accB[:, r * 2048:(r + 1) * 2048]
                kk.v(lambda e, sl=sl: e.tensor_sub(bnc[:, :], sl, mbb[:, :]))
                kk.v(lambda e: e.tensor_mul(bnc[:, :], bnc[:, :], rbb[:, :]),
                     inc="V")
                kk.v(lambda e, r=r: e.tensor_scalar(
                    xslot[:, r * 2048:(r + 1) * 2048], bnc[:, :],
                    m_lnw[:, r:r + 1], m_lnb[:, r:r + 1], ALU.mult, ALU.add),
                    inc="V")
            if "p5u" in dbg:
                kk.w("gpsimd", "V", kk.c["V"])
                kk.g(lambda e: e.dma_start(dbg["p5u"], xslot[:, :]), inc="dW", k=16)
            kk.barrier()
            if "p5" in dbg:
                kk.y(lambda e: e.dma_start(dbg["p5"], accB[:, :]), inc="dX", k=16)
                kk.w("sync", "dX", kk.c["dX"])
                kk.barrier()
            kk.flush()

        if STOP_AFTER not in ("p1", "p2", "p3", "p4", "p5"):
          accB = accA
          with ExitStack() as ph:
            w_ip = [ph.enter_context(nc.sbuf_tensor(f"p6wi{i}", [128, 4096], BF16))
                    for i in range(2)]
            w_op = ph.enter_context(nc.sbuf_tensor("p6wo", [128, 4096], BF16))
            w_xp = ph.enter_context(nc.sbuf_tensor("p6wx", [128, 1536], BF16))
            w_dt = ph.enter_context(nc.sbuf_tensor("p6wd", [64, 2048], BF16))
            xc_s = ph.enter_context(nc.sbuf_tensor("p6xc", [128, 16 * TC], BF16))
            dl_s = ph.enter_context(nc.sbuf_tensor("p6dl", [128, 16 * TC], F32))
            dx_s = ph.enter_context(nc.sbuf_tensor("p6dx", [128, 16 * TC], BF16))
            ya_s = [ph.enter_context(nc.sbuf_tensor(f"p6y{i}", [128, TC], F32))
                    for i in range(2)]
            yb_s = ph.enter_context(nc.sbuf_tensor("p6yb", [128, 16 * TC], BF16))
            hstate = ph.enter_context(nc.sbuf_tensor("p6hs", [128, 256], F32))
            hist = [ph.enter_context(nc.sbuf_tensor(f"p6h{i}", [128, TC], F32))
                    for i in range(2)]
            dAt = [ph.enter_context(nc.sbuf_tensor(f"p6a{i}", [128, TC], F32))
                   for i in range(2)]
            dBu = [ph.enter_context(nc.sbuf_tensor(f"p6b{i}", [128, TC], F32))
                   for i in range(2)]
            bct = ph.enter_context(nc.sbuf_tensor("p6bc", [96, TC], BF16))
            dt16 = ph.enter_context(nc.sbuf_tensor("p6dt", [64, TC], BF16))
            zsl = [ph.enter_context(nc.sbuf_tensor(f"p6z{i}", [128, TC], F32))
                   for i in range(2)]
            cvt = ph.enter_context(nc.sbuf_tensor("p6cv", [128, TC], F32))
            slv = ph.enter_context(nc.sbuf_tensor("p6sl", [128, 64], F32))

            kk.g(lambda e: e.dma_start(
                w_xp.ap().rearrange("q (kc m) -> q kc m", kc=16),
                mxp_in.rearrange("(kc q) m -> q kc m", q=128)), inc="dW", k=16)
            kk.g(lambda e: e.dma_start(w_dt[:, :], mdt_in), inc="dW", k=16)
            smallw = kk.c["dW"]

            kk.v(lambda e: e.memset(hstate[:, :], 0.0))
            kk.v(lambda e: e.memset(slv[:, :], 0.0), inc="V")

            def ld_ip(piece, buf, zhalf, gate):
                m0 = zhalf * 2048 + piece * 512
                kk.w("gpsimd", "T", gate)
                kk.g(lambda e, m0=m0, buf=buf: e.dma_start(
                    w_ip[buf].ap().rearrange("q (kc m) -> q kc m", kc=8),
                    minp_in.rearrange("(kc q) m -> q kc m", q=128)
                    [:, :, m0:m0 + 512]), inc="dW", k=16)
                return kk.c["dW"]

            def ld_op(piece, gate):
                m0 = piece * 256
                kk.w("gpsimd", "T", gate)
                kk.g(lambda e, m0=m0: e.dma_start(
                    w_op.ap().rearrange("q (kc m) -> q kc m", kc=16),
                    mout_in.rearrange("(kc q) m -> q kc m", q=128)
                    [:, :, m0:m0 + 256]), inc="dW", k=16)
                return kk.c["dW"]

            ipld = [ld_ip(0, 0, 0, 0), 0]
            xz_ev = [0, 0]
            dl_ev = [0, 0]
            z_evS = [0, 0]
            zuse = [0, 0]
            opj_ev = [0, 0]
            scan_mS = [0, 0]
            bcused = [0, 0]
            lastT = 0
            dbc_scalar_gate = 0
            silu_m = 0
            dl_add = 0

            for c in range(NCH):
                c0 = c * TC
                for piece in range(4):
                    buf = piece % 2
                    if piece + 1 < 4:
                        ipld[(piece + 1) % 2] = ld_ip(piece + 1, (piece + 1) % 2,
                                                      0, lastT)
                    kk.w("tensor", "dW", ipld[buf])
                    for sub in range(4):
                        dt_i = piece * 4 + sub
                        pt = (pA, pB)[dt_i % 2]
                        kk.w("tensor", "V", xz_ev[dt_i % 2])
                        for kc in range(8):
                            kk.t(lambda e, pt=pt, kc=kc, sub=sub, c0=c0, buf=buf:
                                 e.matmul(
                                     pt[:, 0:TC],
                                     w_ip[buf][:, kc * 512 + sub * 128:
                                               kc * 512 + (sub + 1) * 128],
                                     xslot[:, kc * 2048 + c0:
                                           kc * 2048 + c0 + TC],
                                     start=(kc == 0), stop=(kc == 7)),
                                 inc="T" if kc == 7 else None)
                        lastT = kk.c["T"]
                        mm = kk.c["T"]
                        kk.w("vector", "T", mm)
                        kk.w("vector", "S", silu_m)
                        kk.v(lambda e, pt=pt, dt_i=dt_i: e.tensor_scalar(
                            cvt[:, :], pt[:, 0:TC],
                            m_cw[:, dt_i * 4 + 3: dt_i * 4 + 4], None, ALU.mult))
                        for kcv in range(3):
                            sh = 3 - kcv
                            kk.v(lambda e, pt=pt, dt_i=dt_i, kcv=kcv, sh=sh:
                                 e.scalar_tensor_tensor(
                                     cvt[:, sh:TC], pt[:, 0:TC - sh],
                                     m_cw[:, dt_i * 4 + kcv: dt_i * 4 + kcv + 1],
                                     cvt[:, sh:TC], ALU.mult, ALU.add))
                            kk.v(lambda e, dt_i=dt_i, kcv=kcv, sh=sh:
                                 e.scalar_tensor_tensor(
                                     cvt[:, 0:sh],
                                     slv[:, dt_i * 4 + kcv: dt_i * 4 + kcv + sh],
                                     m_cw[:, dt_i * 4 + kcv: dt_i * 4 + kcv + 1],
                                     cvt[:, 0:sh], ALU.mult, ALU.add))
                        kk.v(lambda e, pt=pt, dt_i=dt_i: e.tensor_copy(
                            slv[:, dt_i * 4: dt_i * 4 + 3], pt[:, TC - 3:TC]),
                            inc="V")
                        xz_ev[dt_i % 2] = kk.c["V"]
                        cv_ok = kk.c["V"]
                        kk.w("scalar", "V", cv_ok)
                        kk.s(lambda e, dt_i=dt_i: e.activation(
                            xc_s[:, dt_i * TC:(dt_i + 1) * TC], cvt[:, :],
                            AF.Silu, bias=m_cb[:, dt_i:dt_i + 1]), inc="S")
                        silu_m = kk.c["S"]
                xc_ok = kk.c["S"]

                if c == 0 and "p6x" in dbg:
                    kk.w("gpsimd", "S", xc_ok)
                    kk.g(lambda e: e.dma_start(dbg["p6x"][:, 0:16 * TC],
                                               xc_s[:, :]), inc="dW", k=16)
                kk.w("tensor", "S", xc_ok)
                kk.w("tensor", "dW", smallw)
                for kc in range(16):
                    kk.t(lambda e, kc=kc: e.matmul(
                        pC[0:96, 0:TC], w_xp[:, kc * 96:(kc + 1) * 96],
                        xc_s[:, kc * TC:(kc + 1) * TC],
                        start=(kc == 0), stop=(kc == 15)),
                        inc="T" if kc == 15 else None)
                dbc_mm = kk.c["T"]
                kk.w("scalar", "T", dbc_mm)
                kk.w("scalar", "T", dbc_scalar_gate)
                kk.s(lambda e: e.copy(dt16[:, :], pC[0:64, 0:TC]))
                kk.s(lambda e: e.copy(bct[64:96, :], pC[64:96, 0:TC]), inc="S")
                dbc_ev = kk.c["S"]

                kk.w("tensor", "S", dbc_ev)
                kk.w("scalar", "V", zuse[0])
                for dt_i in range(16):
                    pt = (pA, pB)[dt_i % 2]
                    kk.w("tensor", "V", dl_ev[dt_i % 2])
                    kk.t(lambda e, pt=pt, dt_i=dt_i: e.matmul(
                        pt[:, 0:TC], w_dt[:, dt_i * 128:(dt_i + 1) * 128],
                        dt16[:, :], start=True, stop=True), inc="T")
                    mm = kk.c["T"]
                    kk.w("scalar", "T", mm)
                    kk.w("scalar", "V", dl_add)
                    kk.s(lambda e, pt=pt, dt_i=dt_i: e.activation(
                        cvt[:, :], pt[:, 0:TC], AF.Abs,
                        bias=m_dtb[:, dt_i:dt_i + 1]))
                    kk.s(lambda e: e.activation(cvt[:, :], cvt[:, :], AF.Exp,
                                                scale=-1.0))
                    kk.s(lambda e: e.activation(zsl[0][:, :], cvt[:, :], AF.Ln,
                                                bias=1.0))
                    kk.s(lambda e, pt=pt, dt_i=dt_i: e.activation(
                        cvt[:, :], pt[:, 0:TC], AF.Relu,
                        bias=m_dtb[:, dt_i:dt_i + 1]), inc="S")
                    sp = kk.c["S"]
                    kk.w("vector", "S", sp)
                    kk.v(lambda e, dt_i=dt_i: e.tensor_add(
                        dl_s[:, dt_i * TC:(dt_i + 1) * TC], cvt[:, :],
                        zsl[0][:, :]), inc="V")
                    dl_ev[dt_i % 2] = kk.c["V"]
                    dl_add = kk.c["V"]

                if c == 0 and "p6d" in dbg:
                    kk.w("sync", "V", kk.c["V"])
                    kk.y(lambda e: e.dma_start(dbg["p6d"][:, 0:16 * TC],
                                               dl_s[:, :]), inc="dX", k=16)
                for dt_i in range(16):
                    kk.v(lambda e, dt_i=dt_i: e.tensor_mul(
                        dx_s[:, dt_i * TC:(dt_i + 1) * TC],
                        dl_s[:, dt_i * TC:(dt_i + 1) * TC],
                        xc_s[:, dt_i * TC:(dt_i + 1) * TC]),
                        inc="V" if dt_i == 15 else None)
                dx_ok = kk.c["V"]

                opl = ld_op(0, lastT)
                zpld = [0, 0]
                for grp in range(4):
                    zpld[grp % 2] = ld_ip(grp, grp % 2, 1, lastT)
                    for sub in range(4):
                        dt_i = grp * 4 + sub
                        ya = ya_s[dt_i % 2]
                        for n in range(NST):
                            pb_, pc_ = ((pE, pR), (pC, pD))[n % 2]
                            kk.w("tensor", "S", dbc_ev)
                            kk.w("tensor", "V", bcused[n % 2])
                            if n == 0:
                                kk.w("tensor", "S", z_evS[0])
                                kk.w("tensor", "S", z_evS[1])
                            kk.t(lambda e, pb_=pb_, n=n: e.matmul(
                                pb_[:, 0:TC],
                                oh32b[64:96, n * 128:(n + 1) * 128],
                                bct[64:96, :], start=True, stop=True))
                            kk.t(lambda e, pc_=pc_, n=n: e.matmul(
                                pc_[:, 0:TC],
                                oh32b[64:96, (16 + n) * 128:(17 + n) * 128],
                                bct[64:96, :], start=True, stop=True), inc="T")
                            bc_mm = kk.c["T"]
                            hb = n % 2
                            kk.w("scalar", "V", scan_mS[hb])
                            kk.s(lambda e, hb=hb, dt_i=dt_i, n=n: e.activation(
                                dAt[hb][:, :],
                                dl_s[:, dt_i * TC:(dt_i + 1) * TC], AF.Exp,
                                scale=m_app[:, dt_i * 16 + n:
                                            dt_i * 16 + n + 1]), inc="S")
                            dA_ok = kk.c["S"]
                            kk.w("vector", "T", bc_mm)
                            if n == 0 and sub == 0 and grp == 0:
                                kk.w("vector", "V", dx_ok)
                            kk.v(lambda e, hb=hb, dt_i=dt_i, pb_=pb_:
                                 e.tensor_mul(
                                     dBu[hb][:, :],
                                     dx_s[:, dt_i * TC:(dt_i + 1) * TC],
                                     pb_[:, 0:TC]))
                            kk.w("vector", "S", dA_ok)
                            kk.v(lambda e, hb=hb: e.tensor_tensor_scan(
                                hist[hb][:, :], dAt[hb][:, :], dBu[hb][:, :],
                                hstate[:, dt_i * 16 + n: dt_i * 16 + n + 1]
                                if False else
                                hstate[:, dt_i * 16 + n: dt_i * 16 + n + 1],
                                ALU.mult, ALU.add))
                            kk.v(lambda e, hb=hb, pc_=pc_: e.tensor_mul(
                                dBu[hb][:, :], hist[hb][:, :], pc_[:, 0:TC]))
                            kk.v(lambda e, hb=hb, dt_i=dt_i, n=n: e.tensor_copy(
                                hstate[:, dt_i * 16 + n: dt_i * 16 + n + 1],
                                hist[hb][:, TC - 1:TC]))
                            if n == 0:
                                kk.v(lambda e, hb=hb, ya=ya: e.tensor_copy(
                                    ya[:, :], dBu[hb][:, :]), inc="V")
                            else:
                                kk.v(lambda e, hb=hb, ya=ya: e.tensor_add(
                                    ya[:, :], ya[:, :], dBu[hb][:, :]), inc="V")
                            scan_mS[hb] = kk.c["V"]
                            bcused[n % 2] = kk.c["V"]
                        kk.v(lambda e, dt_i=dt_i, ya=ya: e.scalar_tensor_tensor(
                            ya[:, :], xc_s[:, dt_i * TC:(dt_i + 1) * TC],
                            m_dpp[:, dt_i:dt_i + 1], ya[:, :],
                            ALU.mult, ALU.add), inc="V")
                        yD_ok = kk.c["V"]
                        zb = dt_i % 2
                        kk.w("tensor", "dW", zpld[grp % 2])
                        kk.w("tensor", "S", z_evS[zb])
                        kk.w("tensor", "V", bcused[0])
                        for kc in range(8):
                            kk.t(lambda e, kc=kc, sub=sub, c0=c0, g2=grp % 2:
                                 e.matmul(
                                     pE[:, 0:TC],
                                     w_ip[g2][:, kc * 512 + sub * 128:
                                              kc * 512 + (sub + 1) * 128],
                                     xslot[:, kc * 2048 + c0:
                                           kc * 2048 + c0 + TC],
                                     start=(kc == 0), stop=(kc == 7)),
                                 inc="T" if kc == 7 else None)
                        lastT = kk.c["T"]
                        zmm = kk.c["T"]
                        kk.w("scalar", "T", zmm)
                        kk.w("scalar", "V", yD_ok)
                        kk.w("scalar", "V", zuse[zb])
                        kk.s(lambda e, zb=zb: e.activation(
                            zsl[zb][:, :], pE[:, 0:TC], AF.Silu), inc="S")
                        z_evS[zb] = kk.c["S"]
                        kk.w("vector", "S", z_evS[zb])
                        kk.v(lambda e, dt_i=dt_i, zb=zb, ya=ya: e.tensor_mul(
                            yb_s[:, dt_i * TC:(dt_i + 1) * TC], ya[:, :],
                            zsl[zb][:, :]), inc="V")
                        zuse[zb] = kk.c["V"]
                yb_ok = kk.c["V"]

                if c == 0 and "p6y" in dbg:
                    kk.w("gpsimd", "V", yb_ok)
                    kk.g(lambda e: e.dma_start(dbg["p6y"][:, 0:16 * TC],
                                               yb_s[:, :]), inc="dW", k=16)
                for piece in range(4):
                    if piece > 0:
                        opl = ld_op(piece, lastT)
                    kk.w("tensor", "dW", opl)
                    kk.w("tensor", "V", yb_ok)
                    for sub in range(2):
                        mc = piece * 2 + sub
                        pt = (pA, pB)[mc % 2]
                        kk.w("tensor", "V", opj_ev[mc % 2])
                        for kc in range(16):
                            kk.t(lambda e, pt=pt, kc=kc, sub=sub: e.matmul(
                                pt[:, 0:TC],
                                w_op[:, kc * 256 + sub * 128:
                                     kc * 256 + (sub + 1) * 128],
                                yb_s[:, kc * TC:(kc + 1) * TC],
                                start=(kc == 0), stop=(kc == 15)),
                                inc="T" if kc == 15 else None)
                        lastT = kk.c["T"]
                        mm = kk.c["T"]
                        kk.w("vector", "T", mm)
                        dst = accB[:, mc * 2048 + c0: mc * 2048 + c0 + TC]
                        kk.v(lambda e, pt=pt, dst=dst: e.tensor_add(
                            dst, dst, pt[:, 0:TC]), inc="V")
                        opj_ev[mc % 2] = kk.c["V"]
                dbc_scalar_gate = lastT
                if c + 1 < NCH:
                    ipld[0] = ld_ip(0, 0, 0, lastT)

            kk.barrier()
            if "p6" in dbg:
                kk.y(lambda e: e.dma_start(dbg["p6"], accB[:, :]), inc="dX", k=16)
                kk.w("sync", "dX", kk.c["dX"])
                kk.barrier()
            kk.flush()

        if STOP_AFTER not in ("p1", "p2", "p3", "p4", "p5", "p6"):
          accB = accA
          with ExitStack() as ph:
            stg = [ph.enter_context(nc.sbuf_tensor(f"p7s{i}", [128, 2048], F32))
                   for i in range(2)]
            rows = ph.enter_context(nc.sbuf_tensor("p7rw", [1, 4096], F32))
            mbb = ph.enter_context(nc.sbuf_tensor("p7mb", [128, 2048], F32))
            rbb = ph.enter_context(nc.sbuf_tensor("p7rb", [128, 2048], F32))
            bnc = ph.enter_context(nc.sbuf_tensor("p7bn", [128, 2048], F32))
            kk.y(lambda e: e.dma_start(rows[0:1, :], scr_rows), inc="dX", k=16)
            rld = kk.c["dX"]
            kk.w("tensor", "dX", rld)
            evS = [0, 0]
            for j, dst in ((0, mbb), (1, rbb)):
                for tg in range(4):
                    pt = (pE, pR)[tg % 2]
                    kk.w("tensor", "S", evS[tg % 2])
                    kk.t(lambda e, pt=pt, j=j, tg=tg: e.matmul(
                        pt[:, :], onesrow[0:1, 0:128],
                        rows[0:1, j * 2048 + tg * 512: j * 2048 + (tg + 1) * 512],
                        start=True, stop=True), inc="T")
                    mt = kk.c["T"]
                    kk.w("scalar", "T", mt)
                    kk.s(lambda e, pt=pt, dst=dst, tg=tg: e.copy(
                        dst[:, tg * 512:(tg + 1) * 512], pt[:, :]), inc="S")
                    evS[tg % 2] = kk.c["S"]
            bc_ok = kk.c["S"]
            kk.w("vector", "S", bc_ok)
            used = [0, 0]
            bnc_free = 0
            for r in range(8):
                b = r % 2
                kk.w("sync", "S", used[b])
                kk.y(lambda e, b=b, r=r: e.dma_start(
                    stg[b][:, :], scr_x1[:, r * 2048:(r + 1) * 2048]),
                    inc="dX", k=16)
                ld = kk.c["dX"]
                kk.w("vector", "dX", ld)
                kk.w("vector", "S", bnc_free)
                kk.v(lambda e, b=b: e.tensor_sub(bnc[:, :], stg[b][:, :],
                                                 mbb[:, :]))
                kk.v(lambda e, b=b: e.tensor_mul(bnc[:, :], bnc[:, :],
                                                 rbb[:, :]), inc="V")
                sv = kk.c["V"]
                kk.w("scalar", "V", sv)
                kk.s(lambda e, r=r: e.copy(
                    xslot[:, r * 2048:(r + 1) * 2048], bnc[:, :]), inc="S")
                used[b] = kk.c["S"]
                bnc_free = kk.c["S"]
            kk.barrier()
            kk.flush()

          with ExitStack() as ph:
            z_sb = ph.enter_context(nc.sbuf_tensor("p7z", [128, 16384], BF16))
            s_sb = ph.enter_context(nc.sbuf_tensor("p7ss", [128, 16384], BF16))
            wbuf = [ph.enter_context(nc.sbuf_tensor(f"p7w{i}", [128, 4096], BF16))
                    for i in range(2)]
            weT = ph.enter_context(nc.sbuf_tensor("p7weT", [8, 2048], F32))
            lg = ph.enter_context(nc.sbuf_tensor("p7lg", [128, 128], F32))
            lgs = ph.enter_context(nc.sbuf_tensor("p7lgs", [128, 128], F32))
            wgw = ph.enter_context(nc.sbuf_tensor("p7gw", [128, 64], BF16))
            tp1 = ph.enter_context(nc.sbuf_tensor("p7t1", [128, 1024], F32))
            tp2 = ph.enter_context(nc.sbuf_tensor("p7t2", [128, 64], F32))
            sml = ph.enter_context(nc.sbuf_tensor("p7sm", [128, 8], F32))

            kk.g(lambda e: e.dma_start(
                wgw.ap().rearrange("q (kc m) -> q kc m", kc=8),
                fgw_in.rearrange("(kc q) m -> q kc m", q=128)), inc="dW", k=16)
            gw_ld = kk.c["dW"]
            kk.w("tensor", "dW", gw_ld)
            lg_ev = [0, 0]
            for tt in range(16):
                pt = (pA, pB)[tt % 2]
                kk.w("tensor", "V", lg_ev[tt % 2])
                for kc in range(8):
                    kk.t(lambda e, pt=pt, kc=kc, tt=tt: e.matmul(
                        pt[:, 0:8],
                        xslot[:, kc * 2048 + tt * 128: kc * 2048 + (tt + 1) * 128],
                        wgw[:, kc * 8:(kc + 1) * 8],
                        start=(kc == 0), stop=(kc == 7)),
                        inc="T" if kc == 7 else None)
                mm = kk.c["T"]
                kk.w("vector", "T", mm)
                kk.v(lambda e, pt=pt, tt=tt: e.tensor_add(
                    lg[:, tt * 8:(tt + 1) * 8], pt[:, 0:8], f_gb[:, :]), inc="V")
                lg_ev[tt % 2] = kk.c["V"]
            for tt in range(16):
                sl = lg[:, tt * 8:(tt + 1) * 8]
                ol = lgs[:, tt * 8:(tt + 1) * 8]
                kk.v(lambda e, sl=sl: e.reduce_max(sml[:, 0:1], sl, AX.X),
                     inc="V")
                kk.w("vector", "V", kk.c["V"])
                kk.v(lambda e, sl=sl: e.tensor_scalar(
                    tp1[:, 0:8], sl, sml[:, 0:1], None, ALU.is_ge), inc="V")
                kk.w("vector", "V", kk.c["V"])
                kk.v(lambda e, sl=sl: e.scalar_tensor_tensor(
                    tp2[:, 0:8], tp1[:, 0:8], -1e30, sl, ALU.mult, ALU.add),
                    inc="V")
                kk.w("vector", "V", kk.c["V"])
                kk.v(lambda e: e.reduce_max(sml[:, 1:2], tp2[:, 0:8], AX.X),
                     inc="V")
                kk.w("vector", "V", kk.c["V"])
                kk.v(lambda e: e.tensor_scalar(
                    tp2[:, 0:8], tp2[:, 0:8], sml[:, 1:2], None, ALU.is_ge),
                    inc="V")
                kk.w("vector", "V", kk.c["V"])
                kk.v(lambda e: e.tensor_sub(sml[:, 2:3], sml[:, 0:1],
                                            sml[:, 1:2]), inc="V")
                dv = kk.c["V"]
                kk.w("scalar", "V", dv)
                kk.s(lambda e: e.activation(sml[:, 3:4], sml[:, 2:3], AF.Sigmoid),
                     inc="S")
                p1_ok = kk.c["S"]
                kk.w("vector", "S", p1_ok)
                kk.v(lambda e: e.tensor_scalar(sml[:, 4:5], sml[:, 3:4], -1.0,
                                               1.0, ALU.mult, ALU.add))
                kk.v(lambda e: e.tensor_scalar(
                    tp1[:, 0:8], tp1[:, 0:8], sml[:, 3:4], None, ALU.mult),
                    inc="V")
                kk.w("vector", "V", kk.c["V"])
                kk.v(lambda e, ol=ol: e.scalar_tensor_tensor(
                    ol, tp2[:, 0:8], sml[:, 4:5], tp1[:, 0:8], ALU.mult,
                    ALU.add), inc="V")
            we_ok = kk.c["V"]
            wtev = [0, 0]
            for tt in range(16):
                pt = (pE, pR)[tt % 2]
                kk.w("tensor", "V", we_ok)
                kk.w("tensor", "V", wtev[tt % 2])
                kk.t(lambda e, pt=pt, tt=tt: e.transpose(
                    pt[0:8, 0:128], lgs[:, tt * 8:(tt + 1) * 8], identf[:, :]),
                    inc="T")
                mt = kk.c["T"]
                kk.w("vector", "T", mt)
                kk.v(lambda e, pt=pt, tt=tt: e.tensor_copy(
                    weT[0:8, tt * 128:(tt + 1) * 128], pt[0:8, 0:128]), inc="V")
                wtev[tt % 2] = kk.c["V"]
            weT_ok = kk.c["V"]

            def ld_piece(src3, e_, piece, kcn, buf, gate):
                kk.w("gpsimd", "T", gate)
                kk.g(lambda en, src3=src3, e_=e_, piece=piece, kcn=kcn, buf=buf:
                     en.dma_start(
                         wbuf[buf].ap().rearrange("q (kc m) -> q kc m", kc=16)
                         [:, 0:kcn, :],
                         src3[e_].rearrange("(kc q) m -> q kc m", q=128)
                         [:, :, piece * 256:(piece + 1) * 256]), inc="dW", k=16)
                return kk.c["dW"]

            wld = [ld_piece(fw1_in, 0, 0, 8, 0, 0), 0]
            x1_ev = [weT_ok] * 3
            z_ev = [weT_ok] * 3
            s_ev = [0, 0, 0, 0]
            w3_ev = [0, 0, 0, 0]
            w3_all = 0
            lastT = 0
            for e_ in range(NE):
                for half in range(2):
                    h0 = half * 1024
                    for _s in s_ev:
                        kk.w("tensor", "S", _s)
                    kk.w("tensor", "V", w3_all)
                    for fc in range(8):
                        b1b = 0
                        b2b = 1
                        if not (e_ == 0 and half == 0 and fc == 0):
                            wld[b1b] = ld_piece(fw1_in, e_, fc, 8, b1b, lastT)
                        wld[b2b] = ld_piece(fw2_in, e_, fc, 8, b2b, lastT)
                        for sub in range(2):
                            fci = fc * 2 + sub
                            for tg in range(2):
                                zi = (fci * 2 + tg) % 3
                                pt = (pA, pB, pC)[zi]
                                pt2 = (pD, pE, pR)[zi]
                                ts1 = tp1[:, (zi % 2) * 512:(zi % 2) * 512 + 512]
                                kk.w("tensor", "dW", wld[b1b])
                                kk.w("tensor", "V", x1_ev[zi])
                                for kc in range(8):
                                    kk.t(lambda en, pt=pt, kc=kc, sub=sub, h0=h0,
                                         tg=tg, b1b=b1b: en.matmul(
                                             pt[:, :],
                                             wbuf[b1b][:, kc * 256 + sub * 128:
                                                       kc * 256 + (sub + 1) * 128],
                                             xslot[:, kc * 2048 + h0 + tg * 512:
                                                   kc * 2048 + h0 + (tg + 1) * 512],
                                             start=(kc == 0), stop=(kc == 7)),
                                         inc="T" if kc == 7 else None)
                                x1mm = kk.c["T"]
                                kk.w("tensor", "dW", wld[b2b])
                                kk.w("tensor", "V", z_ev[zi])
                                for kc in range(8):
                                    kk.t(lambda en, pt2=pt2, kc=kc, sub=sub, h0=h0,
                                         tg=tg, b2b=b2b: en.matmul(
                                             pt2[:, :],
                                             wbuf[b2b][:, kc * 256 + sub * 128:
                                                       kc * 256 + (sub + 1) * 128],
                                             xslot[:, kc * 2048 + h0 + tg * 512:
                                                   kc * 2048 + h0 + (tg + 1) * 512],
                                             start=(kc == 0), stop=(kc == 7)),
                                         inc="T" if kc == 7 else None)
                                lastT = kk.c["T"]
                                x2mm = kk.c["T"]
                                kk.w("vector", "T", x1mm)
                                kk.v(lambda en, pt=pt, e_=e_, fci=fci, ts1=ts1:
                                     en.tensor_scalar(
                                         ts1, pt[:, :],
                                         f_b1[:, e_ * 16 + fci: e_ * 16 + fci + 1],
                                         None, ALU.add), inc="V")
                                x1_ev[zi] = kk.c["V"]
                                kk.w("vector", "T", x2mm)
                                kk.v(lambda en, pt2=pt2, e_=e_, fci=fci, tg=tg,
                                     ts1=ts1: en.scalar_tensor_tensor(
                                         z_sb[:, fci * 1024 + tg * 512:
                                              fci * 1024 + (tg + 1) * 512],
                                         pt2[:, :],
                                         f_b2[:, e_ * 16 + fci: e_ * 16 + fci + 1],
                                         ts1, ALU.add, ALU.mult), inc="V")
                                z_ev[zi] = kk.c["V"]
                    z_ok = kk.c["V"]
                    for gc in range(8):
                        gb_ = gc % 2
                        wld[gb_] = ld_piece(fwg_in, e_, gc, 16, gb_, lastT)
                        kk.w("tensor", "dW", wld[gb_])
                        kk.w("tensor", "V", z_ok)
                        for sub in range(2):
                            gci = gc * 2 + sub
                            for tg in range(2):
                                si = (gci * 2 + tg) % 4
                                pt = (pA, pB, pC, pD)[si]
                                kk.w("tensor", "S", s_ev[si])
                                for kc in range(16):
                                    kk.t(lambda en, pt=pt, kc=kc, sub=sub, tg=tg,
                                         gb_=gb_: en.matmul(
                                             pt[:, :],
                                             wbuf[gb_][:, kc * 256 + sub * 128:
                                                       kc * 256 + (sub + 1) * 128],
                                             z_sb[:, kc * 1024 + tg * 512:
                                                  kc * 1024 + (tg + 1) * 512],
                                             start=(kc == 0), stop=(kc == 15)),
                                         inc="T" if kc == 15 else None)
                                lastT = kk.c["T"]
                                smm = kk.c["T"]
                                kk.w("scalar", "T", smm)
                                kk.s(lambda en, pt=pt, e_=e_, gci=gci, tg=tg:
                                     en.activation(
                                         s_sb[:, gci * 1024 + tg * 512:
                                              gci * 1024 + (tg + 1) * 512],
                                         pt[:, :], AF.Silu,
                                         bias=f_bg[:, e_ * 16 + gci:
                                                   e_ * 16 + gci + 1]), inc="S")
                                s_ev[si] = kk.c["S"]
                    s_ok = kk.c["S"]
                    kk.w("tensor", "V", weT_ok)
                    kk.w("tensor", "V", w3_all)
                    for tg in range(2):
                        kk.t(lambda en, pt=(pE, pR)[tg], e_=e_, tg=tg, h0=h0:
                             en.matmul(
                                 pt[:, :], oh8f[0:8, e_ * 128:(e_ + 1) * 128],
                                 weT[0:8, h0 + tg * 512: h0 + (tg + 1) * 512],
                                 start=True, stop=True), inc="T")
                    web_mm = kk.c["T"]
                    for piece in range(4):
                        pb_ = piece % 2
                        wld[pb_] = ld_piece(fw3_in, e_, piece, 16, pb_, lastT)
                        kk.w("tensor", "dW", wld[pb_])
                        kk.w("tensor", "S", s_ok)
                        for sub in range(2):
                            hc = piece * 2 + sub
                            for tg in range(2):
                                yi = (hc * 2 + tg) % 4
                                pt = (pA, pB, pC, pD)[yi]
                                ts1 = tp1[:, (yi % 2) * 512:(yi % 2) * 512 + 512]
                                kk.w("tensor", "V", w3_ev[yi])
                                for kc in range(16):
                                    kk.t(lambda en, pt=pt, kc=kc, sub=sub, tg=tg,
                                         pb_=pb_: en.matmul(
                                             pt[:, :],
                                             wbuf[pb_][:, kc * 256 + sub * 128:
                                                       kc * 256 + (sub + 1) * 128],
                                             s_sb[:, kc * 1024 + tg * 512:
                                                  kc * 1024 + (tg + 1) * 512],
                                             start=(kc == 0), stop=(kc == 15)),
                                         inc="T" if kc == 15 else None)
                                lastT = kk.c["T"]
                                ymm = kk.c["T"]
                                kk.w("vector", "T", ymm)
                                kk.w("vector", "T", web_mm)
                                kk.v(lambda en, pt=pt, e_=e_, hc=hc, ts1=ts1:
                                     en.tensor_scalar(
                                         ts1, pt[:, :],
                                         f_b3[:, e_ * 8 + hc: e_ * 8 + hc + 1],
                                         None, ALU.add))
                                kk.v(lambda en, wb=(pE, pR)[tg], ts1=ts1:
                                     en.tensor_mul(ts1, ts1, wb[:, :]))
                                dst = accB[:, hc * 2048 + h0 + tg * 512:
                                           hc * 2048 + h0 + (tg + 1) * 512]
                                kk.v(lambda en, dst=dst, ts1=ts1: en.tensor_add(
                                    dst, dst, ts1), inc="V")
                                w3_ev[yi] = kk.c["V"]
                    w3_all = kk.c["V"]
                    if not (e_ == NE - 1 and half == 1):
                        ne_ = e_ + (1 if half == 1 else 0)
                        wld[0] = ld_piece(fw1_in, ne_, 0, 8, 0, lastT)

            kk.barrier()
            if "p7" in dbg:
                kk.y(lambda e: e.dma_start(dbg["p7"], accB[:, :]), inc="dX", k=16)
                kk.w("sync", "dX", kk.c["dX"])
                kk.barrier()
            kk.flush()

        with ExitStack() as ph:
            stg = [ph.enter_context(nc.sbuf_tensor(f"p8s{i}", [128, 2048], F32))
                   for i in range(2)]
            otile = [ph.enter_context(nc.sbuf_tensor(f"p8o{i}", [128, D], F32))
                     for i in range(2)]
            accB = accA
            if STOP_AFTER is None:
                used = [0, 0]
                for r in range(8):
                    b = r % 2
                    kk.w("sync", "V", used[b])
                    kk.y(lambda e, b=b, r=r: e.dma_start(
                        stg[b][:, :], scr_x1[:, r * 2048:(r + 1) * 2048]),
                        inc="dX", k=16)
                    ld = kk.c["dX"]
                    kk.w("vector", "dX", ld)
                    dst = accB[:, r * 2048:(r + 1) * 2048]
                    kk.v(lambda e, b=b, dst=dst: e.tensor_add(
                        dst, dst, stg[b][:, :]), inc="V")
                    used[b] = kk.c["V"]
                sum_ok = kk.c["V"]
            else:
                sum_ok = 0
            tev = [0, 0]
            ost = [0, 0]
            for tt in range(16):
                ob = tt % 2
                kk.w("vector", "dX", ost[ob])
                for r in range(8):
                    pt = (pA, pB)[r % 2]
                    kk.w("tensor", "V", sum_ok)
                    kk.w("tensor", "V", tev[r % 2])
                    kk.t(lambda e, pt=pt, r=r, tt=tt: e.transpose(
                        pt[:, 0:128],
                        accB[:, r * 2048 + tt * 128: r * 2048 + (tt + 1) * 128],
                        identf[:, :]), inc="T")
                    td = kk.c["T"]
                    kk.w("vector", "T", td)
                    kk.v(lambda e, pt=pt, r=r, ob=ob: e.tensor_copy(
                        otile[ob][:, r * 128:(r + 1) * 128], pt[:, 0:128]),
                        inc="V")
                    tev[r % 2] = kk.c["V"]
                row_ok = kk.c["V"]
                kk.w("sync", "V", row_ok)
                kk.y(lambda e, ob=ob, tt=tt: e.dma_start(
                    out_d[tt * 128:(tt + 1) * 128, :], otile[ob][:, :]),
                    inc="dX", k=16)
                ost[ob] = kk.c["dX"]
            kk.w("sync", "dX", kk.c["dX"])
            kk.flush()

    return nc



_NC_CACHE = {}


def _shared_inputs(i):
    f32 = np.float32

    def cc(a):
        return np.ascontiguousarray(np.asarray(a), dtype=f32)

    def pp(a, nchunk):
        return cc(np.asarray(a).reshape(nchunk, 128).T)

    import ml_dtypes

    sh = {}
    sh["ident"] = np.eye(128, dtype=f32)
    oh32 = np.zeros((96, 32 * 128), dtype=f32)
    for r in range(32):
        oh32[64 + r, r * 128:(r + 1) * 128] = 1.0
    sh["oh32f"] = oh32
    oh8 = np.zeros((8, 8 * 128), dtype=f32)
    for r in range(8):
        oh8[r, r * 128:(r + 1) * 128] = 1.0
    sh["oh8f"] = oh8

    sh["l0_wq"] = cc(i["L0_attn_Wq"])
    sh["l0_wk"] = cc(i["L0_attn_Wk"])
    sh["l0_wv"] = cc(i["L0_attn_Wv"])
    sh["l0_wo"] = cc(i["L0_attn_Wo"])
    sh["l0_bo_pp"] = pp(i["L0_attn_bo"], 8)

    gup = np.asarray(i["L0_ffn_gate_up_proj"])
    perm = np.concatenate([np.arange(0, 2048, 2), np.arange(1, 2048, 2)])
    sh["moe_gup"] = cc(gup[:, :, perm])
    gub = np.asarray(i["L0_ffn_gate_up_proj_bias"])[:, perm]
    sh["moe_gub_pp"] = cc(np.stack([gub[e].reshape(16, 128).T for e in range(8)],
                                   axis=1).reshape(128, 8 * 16))
    sh["moe_down"] = cc(i["L0_ffn_down_proj"])
    dbn = np.asarray(i["L0_ffn_down_proj_bias"])
    sh["moe_db_pp"] = cc(np.stack([dbn[e].reshape(8, 128).T for e in range(8)],
                                  axis=1).reshape(128, 8 * 8))
    ridx = np.asarray(i["L0_ffn_router_indices"])
    sh["moe_ridx"] = np.ascontiguousarray(
        ridx.reshape(16, 128, 2).transpose(1, 0, 2).reshape(128, 32),
        dtype=np.int32)
    rw = np.asarray(i["L0_ffn_routing_weights"])
    sh["moe_rw"] = cc(rw.reshape(16, 128, 8).transpose(1, 0, 2).reshape(128, 128))

    sh["m_lnw_pp"] = pp(i["L1_attn_ln_w"], 8)
    sh["m_lnb_pp"] = pp(i["L1_attn_ln_b"], 8)
    sh["m_inproj_t"] = cc(np.asarray(i["L1_attn_in_proj_w"]).T)
    cw = np.asarray(i["L1_attn_conv_w"]).reshape(2048, 4)
    sh["m_convw_pp"] = cc(cw.reshape(16, 128, 4).transpose(1, 0, 2).reshape(128, 64))
    sh["m_convb_pp"] = pp(i["L1_attn_conv_b"], 16)
    sh["m_xproj_t"] = cc(np.asarray(i["L1_attn_x_proj_w"]).T)
    sh["m_dtproj_t"] = cc(np.asarray(i["L1_attn_dt_proj_w"]).T)
    sh["m_dtb_pp"] = pp(i["L1_attn_dt_proj_b"], 16)
    al = np.asarray(i["L1_attn_A_log"])
    sh["m_alog_pp"] = cc(al.reshape(16, 128, 16).transpose(1, 0, 2).reshape(128, 256))
    sh["m_d_pp"] = pp(i["L1_attn_D"], 16)
    sh["m_outproj_t"] = cc(np.asarray(i["L1_attn_out_proj_w"]).T)

    sh["f_gw"] = cc(i["L1_ffn_gate_w"])
    sh["f_gb_t"] = cc(np.broadcast_to(np.asarray(i["L1_ffn_gate_b"]), (128, 8)))
    sh["f_w1"] = cc(i["L1_ffn_w1"])
    sh["f_w2"] = cc(i["L1_ffn_w2"])
    sh["f_wg"] = cc(i["L1_ffn_wg"])
    sh["f_w3"] = cc(i["L1_ffn_w3"])

    def ppk(b, nchunk):
        b = np.asarray(b)
        return cc(np.stack([b[e].reshape(nchunk, 128).T for e in range(8)],
                           axis=1).reshape(128, 8 * nchunk))

    sh["f_b1_pp"] = ppk(i["L1_ffn_b1"], 16)
    sh["f_b2_pp"] = ppk(i["L1_ffn_b2"], 16)
    sh["f_bg_pp"] = ppk(i["L1_ffn_bg"], 16)
    sh["f_b3_pp"] = ppk(i["L1_ffn_b3"], 8)
    return sh


def kernel(**inputs):
    key = (STOP_AFTER, tuple(DEBUG_DUMPS))
    if key not in _NC_CACHE:
        _NC_CACHE[key] = build()
    nc = _NC_CACHE[key]

    sh = _shared_inputs(inputs)
    x_all = np.ascontiguousarray(np.asarray(inputs["L0_attn_x"]), dtype=np.float32)
    in_maps = []
    for c in range(8):
        d = dict(sh)
        d["x"] = np.ascontiguousarray(x_all[c])
        d["moe_mask"] = np.full((128, 1), 1.0 if c == 0 else 0.0, dtype=np.float32)
        in_maps.append(d)

    res = run_bass_kernel_spmd(nc, in_maps, core_ids=list(range(8)))
    out = np.stack([res.results[c]["out"] for c in range(8)], axis=0)
    if DEBUG_DUMPS:
        kernel.last_debug = [
            {nm: res.results[c]["dbg_" + nm] for nm in DEBUG_DUMPS}
            for c in range(8)]
    return out.astype(np.float32)
